# version 13
# baseline (speedup 1.0000x reference)
import numpy as np
import ml_dtypes
from contextlib import ExitStack

import concourse.bass as bass
import concourse.mybir as mybir
from concourse.bass_utils import run_bass_kernel_spmd

F32 = mybir.dt.float32
BF16 = mybir.dt.bfloat16
AF = mybir.ActivationFunctionType
ALU = mybir.AluOpType
AX = mybir.AxisListType

D = 1024
S = 4096
NH, NG, DK = 16, 4, 64
E, F = 8, 4096
EPS = 1e-5
NC = 8
CC = D // 128
FH = F // 2
NU = 2 * E
T1 = 256
NT1 = S // T1
T3 = 512
NT3 = S // T3
VW = NG * 65

ENGS = ("t", "s", "v", "g", "y")


class B:
    def __init__(self, nc, es):
        self.nc = nc
        self.q = {k: [] for k in ENGS}
        self.sems = {}
        self.cnt = {}
        self.waited = {k: {} for k in ENGS}
        for ch in ("T", "Sc", "V", "G", "DS", "DG"):
            self.sems[ch] = es.enter_context(nc.semaphore(ch))
            self.cnt[ch] = 0

    def now(self, ch):
        return self.cnt[ch]

    def snap(self):
        return dict(self.cnt)

    def emit(self, eng, fn, inc=None, k=1, waits=()):
        rw = []
        wd = self.waited[eng]
        for ch, v in waits:
            v = self.cnt[ch] if v is None else v
            if v > 0 and wd.get(ch, 0) < v:
                wd[ch] = v
                rw.append((self.sems[ch], v))
        if inc:
            self.cnt[inc] += k
        sem = self.sems[inc] if inc else None

        def run(e, rw=rw, fn=fn, sem=sem, k=k):
            for h, v in rw:
                e.wait_ge(h, v)
            ins = fn(e)
            if sem is not None:
                ins.then_inc(sem, k)
        self.q[eng].append(run)

    def flush(self, blk):
        qs = {k: list(v) for k, v in self.q.items()}
        for k in self.q:
            self.q[k] = []

        @blk.tensor
        def _(t):
            for fn in qs["t"]:
                fn(t)

        @blk.scalar
        def _(s):
            for fn in qs["s"]:
                fn(s)

        @blk.vector
        def _(v):
            for fn in qs["v"]:
                fn(v)

        @blk.gpsimd
        def _(g):
            for fn in qs["g"]:
                fn(g)

        @blk.sync
        def _(y):
            for fn in qs["y"]:
                fn(y)


def build_program(expose_scratch=False, phases=(1, 2, 3)):
    nc = bass.Bass("TRN2", target_bir_lowering=False)
    dt = nc.dram_tensor
    xTp = dt("xTp", [D, S], F32, kind="ExternalInput").ap()
    wqT = dt("wqT", [D, D], F32, kind="ExternalInput").ap()
    wkT = dt("wkT", [D, NG * DK], F32, kind="ExternalInput").ap()
    wvT = dt("wvT", [D, NG * DK], F32, kind="ExternalInput").ap()
    woTp = dt("woTp", [D, D], F32, kind="ExternalInput").ap()
    cenTp = dt("cenTp", [D, E], F32, kind="ExternalInput").ap()
    gamT = dt("gamT", [E, 128, CC], F32, kind="ExternalInput").ap()
    betT = dt("betT", [E, 128, CC], F32, kind="ExternalInput").ap()
    b1T = dt("b1T", [E, 128, F // 128], F32, kind="ExternalInput").ap()
    b2fm = dt("b2fm", [E, D], F32, kind="ExternalInput").ap()
    w1d = dt("w1d", [E, D, F], F32, kind="ExternalInput").ap()
    w2d = dt("w2d", [E, F, D], F32, kind="ExternalInput").ap()
    cosT = dt("cosT", [D // 2, S], F32, kind="ExternalInput").ap()
    sinT = dt("sinT", [D // 2, S], F32, kind="ExternalInput").ap()
    out = dt("out", [D, S], F32, kind="ExternalOutput").ap()

    skind = "ExternalOutput" if expose_scratch else "Internal"
    xhat_d = dt("xhat_d", [D, S], BF16, kind=skind).ap()
    am_d = dt("am_d", [E, S], F32, kind=skind).ap()
    CAP = 768
    dbgkind = "ExternalOutput" if expose_scratch == 4 else "Internal"
    xs_d = dt("xs_d", [E * CAP, D], BF16, kind=dbgkind).ap()
    term_d = dt("term_d", [E * CAP, D], BF16, kind=dbgkind).ap()
    if expose_scratch >= 2:
        qdbg = dt("qdbg", [128, CC * S], BF16, kind="ExternalOutput").ap()
        kdbg = dt("kdbg", [128, NG * S], BF16, kind="ExternalOutput").ap()
        vdbg = dt("vdbg", [128, 32 * VW], BF16, kind="ExternalOutput").ap()
    if expose_scratch == 3:
        sdbg = dt("sdbg", [128, 1024], F32, kind="ExternalOutput").ap()
        pdbg = dt("pdbg", [128, 1024], BF16, kind="ExternalOutput").ap()
        odbg = dt("odbg", [65, 1024], F32, kind="ExternalOutput").ap()
    row_d = dt("row_d", [4, S], F32, kind="Internal").ap()

    out_r = out.rearrange("(c p) t -> p c t", p=128)
    xhat_r = xhat_d.rearrange("(c p) t -> p c t", p=128)
    xTp_r = xTp.rearrange("(c p) t -> p c t", p=128)
    cosT_r = cosT.rearrange("(c p) t -> p c t", p=128)
    sinT_r = sinT.rearrange("(c p) t -> p c t", p=128)

    es = ExitStack()
    b = B(nc, es)
    SB = lambda n, sh, d2: es.enter_context(nc.sbuf_tensor(n, sh, d2))
    blk = es.enter_context(nc.Block())

    ones_f = SB("ones_f", [128, 1], F32)
    ones_r = SB("ones_r", [1, 128], F32)
    ident = SB("ident", [128, 128], F32)
    cen_sb = SB("cen_sb", [128, CC * E], F32)
    b2_sb = SB("b2_sb", [E, D], F32)

    eps_sb = SB("eps_sb", [1, 1], F32)
    eps2_sb = SB("eps2_sb", [1, 1], F32)
    b.emit("g", lambda g: g.memset(eps_sb[:], EPS), inc="G")
    b.emit("g", lambda g: g.memset(eps2_sb[:], EPS * EPS), inc="G")
    b.emit("g", lambda g: g.memset(ones_f[:], 1.0), inc="G")
    b.emit("g", lambda g: g.memset(ones_r[:], 1.0), inc="G")
    b.emit("g", lambda g: g.memset(ident[:], 1.0), inc="G")
    b.emit("g", lambda g: g.affine_select(
        out=ident[:], in_=ident[:], pattern=[[-1, 128]],
        compare_op=ALU.is_equal, fill=0.0, base=0, channel_multiplier=1),
        inc="G", waits=[("G", None)])
    b.emit("y", lambda y: y.dma_start(
        out=cen_sb[:].rearrange("p (c e) -> p c e", e=E),
        in_=cenTp.rearrange("(c p) e -> p c e", p=128)), inc="DS", k=16)
    b.emit("y", lambda y: y.dma_start(out=b2_sb[:], in_=b2fm), inc="DS", k=16)
    CONST_G = b.now("G")
    ident_bf = SB("ident_bf", [128, 128], BF16)
    b.emit("v", lambda v: v.tensor_copy(ident_bf[:], ident[:]),
           inc="V", waits=[("G", CONST_G)])
    CONST_V = b.now("V")
    CONST_DS = b.now("DS")

    s12 = ExitStack()
    q_sb = s12.enter_context(nc.sbuf_tensor("q_sb", [128, CC * S], BF16))
    k_sb = s12.enter_context(nc.sbuf_tensor("k_sb", [128, NG * S], BF16))
    v_sb = s12.enter_context(nc.sbuf_tensor("v_sb", [128, 32 * VW], BF16))

    p1c = p2c = None

    if 1 in phases:
        with ExitStack() as s1:
            SB1 = lambda n, sh, d2: s1.enter_context(nc.sbuf_tensor(n, sh, d2))
            PS1 = lambda n, sh: s1.enter_context(nc.psum_tensor(n, sh, F32))
            wq_sb = SB1("wq_sb", [128, CC * D], BF16)
            wk_sb = SB1("wk_sb", [128, CC * NG * DK], BF16)
            wv_sb = SB1("wv_sb", [128, CC * NG * DK], BF16)
            x_sb = [SB1(f"x_sb{i}", [128, CC * T1], F32) for i in range(2)]
            xn32 = SB1("xn32", [128, CC * T1], F32)
            scr = SB1("scr", [128, CC * T1], F32)
            xhbf = SB1("xhbf", [128, CC * T1], BF16)
            sq_sb = SB1("sq_sb", [128, CC * T1], F32)
            xr_t2 = [SB1(f"xr_t{i}", [128, CC * T1], BF16) for i in range(2)]
            cs_sb = SB1("cs_sb", [128, 2 * T1], F32)
            rows = SB1("rows", [1, 7 * T1], F32)
            sc_sb = SB1("sc_sb", [E, T1], F32)
            sctm = SB1("sctm", [128, E], F32)
            sctm_b = SB1("sctm_b", [128, E], F32)
            onez = sctm
            onez_b = sctm_b
            rq = SB1("rq", [128, 3], F32)
            rq_b = SB1("rq_b", [128, 3], F32)
            amrow = SB1("amrow", [E, 128], F32)
            amrow_b = SB1("amrow_b", [E, 128], F32)
            ps_s12 = PS1("ps_s12", [1, 2 * T1])
            ps_s1 = ps_s12[0:1, 0:T1]
            ps_s2 = ps_s12[0:1, T1:2 * T1]
            ps_tr2 = PS1("ps_tr2", [128, 512])
            ps_sc = PS1("ps_sc", [E, T1])
            ps_bc = PS1("ps_bc1", [128, 1024])
            ps_pj = PS1("ps_pj", [128, 1024])
            ps_v = PS1("ps_v", [128, 512])
            ps_tr = ps_v

            b.emit("g", lambda g: g.dma_start(
                out=wq_sb[:].rearrange("p (c d) -> p c d", d=D),
                in_=wqT.rearrange("(c p) d -> p c d", p=128)), inc="DG", k=16)
            b.emit("g", lambda g: g.dma_start(
                out=wk_sb[:].rearrange("p (c d) -> p c d", d=NG * DK),
                in_=wkT.rearrange("(c p) d -> p c d", p=128)), inc="DG", k=16)
            b.emit("g", lambda g: g.dma_start(
                out=wv_sb[:].rearrange("p (c d) -> p c d", d=NG * DK),
                in_=wvT.rearrange("(c p) d -> p c d", p=128)), inc="DG", k=16)
            b.emit("g", lambda g: g.memset(v_sb[:], 1.0), inc="G")
            WCAST_DG = b.now("DG")
            VONE_G = b.now("G")

            R_MU, R_R, R_SM, R_MSQ, R_VAR, R_A, R_R2 = 0, 1, 1, 3, 4, 5, 6
            rsl = lambda i: (slice(0, 1), slice(i * T1, (i + 1) * T1))

            csl = [slice(c * T1, (c + 1) * T1) for c in range(CC)]
            x_users_done = [dict(V=0, T=0, Sc=0), dict(V=0, T=0, Sc=0)]
            pj_prev = [("Sc", 0), ("Sc", 0)]
            bc_users = [0, 0]
            prev_s2b = 0
            xh_dma_prev = 0
            ps_free = 0
            scv_prev = 0
            outb_prev = 0
            rope_prev = 0
            pending_proj = None

            for tb in range(NT1):
                par = tb % 2
                xs = x_sb[par]
                tsl = slice(tb * T1, (tb + 1) * T1)
                ud = x_users_done[par]
                b.emit("y", lambda y, o=xs[:].rearrange("p (c t) -> p c t", t=T1),
                       i=xTp_r[:, :, tsl]: y.dma_start(out=o, in_=i),
                       inc="DS", k=16,
                       waits=[("V", ud["V"]), ("T", ud["T"]), ("Sc", ud["Sc"])])
                ld_x = b.now("DS")

                for c in range(CC):
                    b.emit("s", lambda s, o=sq_sb[:, csl[c]], i=xs[:, csl[c]]:
                           s.activation(out=o, in_=i, func=AF.Square),
                           inc="Sc", waits=[("DS", ld_x), ("T", prev_s2b)])
                    b.emit("t", lambda t, o=ps_s1, l=ones_f[:, 0:1], r=xs[:, csl[c]],
                           st=(c == 0), sp=(c == CC - 1):
                           t.matmul(o, l, r, start=st, stop=sp),
                           inc=("T" if c == CC - 1 else None),
                           waits=[("DS", ld_x), ("V", ps_free)])
                sq1 = b.now("Sc")
                s1_d = b.now("T")
                for c in range(CC):
                    b.emit("t", lambda t, o=ps_s2, l=ones_f[:, 0:1], r=sq_sb[:, csl[c]],
                           st=(c == 0), sp=(c == CC - 1):
                           t.matmul(o, l, r, start=st, stop=sp),
                           inc=("T" if c == CC - 1 else None), waits=[("Sc", sq1)])
                s2_d = b.now("T")

                b.emit("v", lambda v, o=rows[0:1, 0:2 * T1], i=ps_s12[0:1, :]:
                       v.tensor_scalar_mul(o, i, 1.0 / D),
                       inc="V", waits=[("T", s2_d)])
                muV = b.now("V")
                ps_free = muV
                b.emit("s", lambda s, o=rows[rsl(R_MSQ)], i=rows[rsl(R_MU)]:
                       s.activation(out=o, in_=i, func=AF.Square),
                       inc="Sc", waits=[("V", muV)])
                b.emit("v", lambda v, o=rows[rsl(R_VAR)], a=rows[rsl(R_SM)], c2=rows[rsl(R_MSQ)]:
                       v.tensor_sub(o, a, c2),
                       inc="V", waits=[("Sc", None), ("V", None)])
                v1_v = b.now("V")
                b.emit("s", lambda s, o=rows[rsl(R_A)], i=rows[rsl(R_VAR)]:
                       s.activation(out=o, in_=i, func=AF.Ln, bias=eps_sb[0:1, 0:1]),
                       inc="Sc", waits=[("V", v1_v), ("G", CONST_G)])
                b.emit("s", lambda s, o=rows[rsl(R_R)], i=rows[rsl(R_A)]:
                       s.activation(out=o, in_=i, func=AF.Exp, scale=-0.5),
                       inc="Sc", waits=[("Sc", None)])
                rS = b.now("Sc")
                b.emit("s", lambda s, o=rows[rsl(R_R2)], i=rows[rsl(R_VAR)]:
                       s.activation(out=o, in_=i, func=AF.Ln, scale=1.0 + EPS,
                                    bias=eps2_sb[0:1, 0:1]),
                       inc="Sc", waits=[("V", v1_v)])
                b.emit("v", lambda v, o=rows[rsl(R_R2)], a=rows[rsl(R_A)], c2=rows[rsl(R_R2)]:
                       v.tensor_sub(o, a, c2),
                       inc="V", waits=[("Sc", None)])
                b.emit("s", lambda s, o=rows[rsl(R_R2)], i=rows[rsl(R_R2)]:
                       s.activation(out=o, in_=i, func=AF.Exp, scale=0.5),
                       inc="Sc", waits=[("V", None)])
                r2S = b.now("Sc")
                b.emit("t", lambda t, o=ps_bc[:, 0:2 * T1], l=ones_r[0:1, :],
                       r=rows[0:1, 0:2 * T1]:
                       t.matmul(o, l, r, start=True, stop=True),
                       inc="T", waits=[("V", muV), ("Sc", rS), ("V", bc_users[0]),
                                       ("G", CONST_G)])
                bc1T = b.now("T")

                for c in range(CC):
                    b.emit("v", lambda v, o=scr[:, csl[c]], a=xs[:, csl[c]], c2=ps_bc[:, 0:T1]:
                           v.tensor_sub(o, a, c2),
                           inc="V", waits=[("T", bc1T), ("DG", outb_prev),
                                           ("G", rope_prev)])
                    b.emit("v", lambda v, o=xn32[:, csl[c]], a=scr[:, csl[c]], c2=ps_bc[:, T1:2 * T1] if True else None:
                           v.tensor_mul(o, a, c2),
                           inc="V", waits=[("V", None)])
                xn_d = b.now("V")
                bc_users[0] = xn_d
                ud["V"] = xn_d

                for c in range(CC):
                    b.emit("t", lambda t, o=ps_sc[:, :], l=cen_sb[:, c * E:(c + 1) * E],
                           r=xn32[:, csl[c]], st=(c == 0), sp=(c == CC - 1):
                           t.matmul(o, l, r, start=st, stop=sp),
                           inc=("T" if c == CC - 1 else None),
                           waits=[("V", xn_d), ("DS", CONST_DS), ("V", scv_prev)])
                sc_d = b.now("T")
                b.emit("t", lambda t, o=ps_bc[:, 512:512 + T1], l=ones_r[0:1, :],
                       r=rows[rsl(R_R2)]:
                       t.matmul(o, l, r, start=True, stop=True),
                       inc="T", waits=[("Sc", r2S), ("V", bc_users[1])])
                bc2T = b.now("T")

                for c in range(CC):
                    b.emit("v", lambda v, o=xhbf[:, csl[c]], a=xn32[:, csl[c]], c2=ps_bc[:, 512:512 + T1]:
                           v.tensor_mul(o, a, c2),
                           inc="V", waits=[("T", bc2T), ("DS", xh_dma_prev)])
                xh_d = b.now("V")
                bc_users[1] = xh_d
                b.emit("y", lambda y, o=xhat_r[:, :, tsl], i=xhbf[:].rearrange("p (c t) -> p c t", t=T1):
                       y.dma_start(out=o, in_=i), inc="DS", k=16, waits=[("V", xh_d)])
                xh_dma_prev = b.now("DS")

                for c in range(CC):
                    b.emit("v", lambda v, o=scr[:, csl[c]], a=xs[:, csl[c]], c2=xn32[:, csl[c]]:
                           v.tensor_add(o, a, c2),
                           inc="V", waits=[])
                ox_d = b.now("V")
                ud["V"] = ox_d
                b.emit("g", lambda g2, o=out_r[:, :, tsl], i=scr[:].rearrange("p (c t) -> p c t", t=T1):
                       g2.dma_start(out=o, in_=i), inc="DG", k=16, waits=[("V", ox_d)])
                outb_dg = b.now("DG")

                xr_t = xr_t2[par]
                for j in range(4):
                    ja, jb = csl[j], csl[j + 4]
                    co, si = slice(0, T1), slice(T1, 2 * T1)
                    b.emit("y", lambda y, o=cs_sb[:, 0:T1], i=cosT[j * 128:(j + 1) * 128, tsl]:
                           y.dma_start(out=o, in_=i), inc="DS", k=16, waits=[("G", None)])
                    b.emit("y", lambda y, o=cs_sb[:, T1:2 * T1], i=sinT[j * 128:(j + 1) * 128, tsl]:
                           y.dma_start(out=o, in_=i), inc="DS", k=16)
                    cs_ld = b.now("DS")
                    b.emit("g", lambda g2, o=scr[:, ja], a=xn32[:, ja], c2=cs_sb[:, co]:
                           g2.tensor_mul(o, a, c2),
                           inc="G", waits=[("DS", cs_ld), ("T", None), ("DG", outb_dg),
                                           ("V", ox_d)])
                    b.emit("g", lambda g2, o=xr_t[:, ja], a=xn32[:, jb], c2=cs_sb[:, si]:
                           g2.tensor_mul(o, a, c2), inc="G")
                    b.emit("g", lambda g2, o=xr_t[:, ja], a=scr[:, ja], c2=xr_t[:, ja]:
                           g2.tensor_sub(o, a, c2), inc="G", waits=[("G", None)])
                    b.emit("g", lambda g2, o=scr[:, ja], a=xn32[:, ja], c2=cs_sb[:, si]:
                           g2.tensor_mul(o, a, c2), inc="G", waits=[("G", None)])
                    b.emit("g", lambda g2, o=xr_t[:, jb], a=xn32[:, jb], c2=cs_sb[:, co]:
                           g2.tensor_mul(o, a, c2), inc="G")
                    b.emit("g", lambda g2, o=xr_t[:, jb], a=scr[:, ja], c2=xr_t[:, jb]:
                           g2.tensor_add(o, a, c2), inc="G", waits=[("G", None)])
                rope_d = b.now("G")
                rope_prev = rope_d

                b.emit("v", lambda v, o=sc_sb[:, :], i=ps_sc[:, :]: v.tensor_copy(o, i),
                       inc="V", waits=[("T", sc_d)])
                scv = b.now("V")
                qs_ = [slice(0, 128), slice(128, 256)]
                trA = [ps_tr[:, 0:E], ps_tr2[:, 0:E]]
                trB = [ps_tr[0:E, 256:384], ps_tr2[0:E, 256:384]]
                sct = [sctm, sctm_b]
                onz = [onez, onez_b]
                rqq = [rq, rq_b]
                amr = [amrow, amrow_b]
                for q_ in range(2):
                    b.emit("t", lambda t, o=trA[q_], i=sc_sb[:, qs_[q_]], idd=ident[0:E, 0:E]:
                           t.transpose(o, i, idd),
                           inc="T", waits=[("V", None), ("G", CONST_G)])
                for q_ in range(2):
                    b.emit("v", lambda v, o=sct[q_][:, :], i=trA[q_]: v.tensor_copy(o, i),
                           inc="V", waits=[("T", None), ("V", None)])
                for q_ in range(2):
                    b.emit("v", lambda v, o=rqq[q_][:, 0:1], i=sct[q_][:, :]:
                           v.reduce_max(o, i, axis=AX.X), inc="V", waits=[("V", None)])
                mxv = b.now("V")
                for q_ in range(2):
                    b.emit("v", lambda v, o=onz[q_][:, :], i=sct[q_][:, :], m=rqq[q_][:, 0:1]:
                           v.tensor_scalar(out=o, in0=i, scalar1=m, scalar2=None, op0=ALU.is_ge),
                           inc="V", waits=[("V", None)])
                for q_ in range(2):
                    b.emit("s", lambda s, o=rqq[q_][:, 2:3], i=rqq[q_][:, 0:1]:
                           s.activation(out=o, in_=i, func=AF.Exp, scale=-1.0),
                           inc="Sc", waits=[("V", mxv)])
                for q_ in range(2):
                    b.emit("v", lambda v, o=rqq[q_][:, 2:3], i=rqq[q_][:, 2:3]:
                           v.tensor_scalar_add(o, i, 1.0),
                           inc="V", waits=[("Sc", None), ("V", None)])
                for q_ in range(2):
                    b.emit("v", lambda v, o=rqq[q_][:, 1:2], i=rqq[q_][:, 2:3]:
                           v.reciprocal(o, i), inc="V", waits=[("V", None)])
                for q_ in range(2):
                    b.emit("v", lambda v, o=onz[q_][:, :], i=onz[q_][:, :], m=rqq[q_][:, 1:2]:
                           v.tensor_scalar_mul(o, i, m),
                           inc="V", waits=[("V", None)])
                for q_ in range(2):
                    b.emit("t", lambda t, o=trB[q_], i=onz[q_][:, :], idd=ident[:, :]:
                           t.transpose(o, i, idd), inc="T", waits=[("V", None)])
                for q_ in range(2):
                    b.emit("v", lambda v, o=amr[q_][:, :], i=trB[q_]:
                           v.tensor_copy(o, i), inc="V", waits=[("T", None), ("DS", None)])
                for q_ in range(2):
                    b.emit("y", lambda y, o=am_d[0:E, tb * T1 + q_ * 128: tb * T1 + (q_ + 1) * 128],
                           i=amr[q_][:, :]: y.dma_start(out=o, in_=i),
                           inc="DS", k=16, waits=[("V", None)])

                def mk_proj(tb=tb, xr_t=xr_t, rope_d=rope_d):
                  def emit_proj():
                    for dq in range(CC):
                        half = dq % 2
                        osl = slice(half * 512, half * 512 + T1)
                        for c in range(CC):
                            w0 = []
                            if c == 0:
                                w0 = [("G", rope_d), ("DG", WCAST_DG), pj_prev[half]]
                            b.emit("t", lambda t, o=ps_pj[:, osl],
                                   l=wq_sb[:, c * D + dq * 128: c * D + (dq + 1) * 128],
                                   r=xr_t[:, csl[c]], st=(c == 0), sp=(c == CC - 1):
                                   t.matmul(o, l, r, start=st, stop=sp),
                                   inc=("T" if c == CC - 1 else None), waits=w0)
                        b.emit("s", lambda s, o=q_sb[:, dq * S + tb * T1: dq * S + (tb + 1) * T1],
                               i=ps_pj[:, osl]: s.copy(o, i),
                               inc="Sc", waits=[("T", None)])
                        pj_prev[half] = ("Sc", b.now("Sc"))
                    for g2 in range(NG):
                        half = g2 % 2
                        osl = slice(half * 512, half * 512 + T1)
                        for c in range(CC):
                            w0 = [pj_prev[half]] if c == 0 else []
                            b.emit("t", lambda t, o=ps_pj[0:64, osl],
                                   l=wk_sb[:, c * (NG * DK) + g2 * 64: c * (NG * DK) + (g2 + 1) * 64],
                                   r=xr_t[:, csl[c]], st=(c == 0), sp=(c == CC - 1):
                                   t.matmul(o, l, r, start=st, stop=sp),
                                   inc=("T" if c == CC - 1 else None), waits=w0)
                        b.emit("s", lambda s, o=k_sb[0:64, g2 * S + tb * T1: g2 * S + (tb + 1) * T1],
                               i=ps_pj[0:64, osl]: s.copy(o, i),
                               inc="Sc", waits=[("T", None)])
                        pj_prev[half] = ("Sc", b.now("Sc"))
                    for tt in range(T1 // 128):
                        gt = tb * (T1 // 128) + tt
                        for c in range(CC):
                            b.emit("t", lambda t, o=ps_v[:, 0:256],
                                   l=xr_t[:, c * T1 + tt * 128: c * T1 + (tt + 1) * 128],
                                   r=wv_sb[:, c * 256:(c + 1) * 256],
                                   st=(c == 0), sp=(c == CC - 1):
                                   t.matmul(o, l, r, start=st, stop=sp),
                                   inc=("T" if c == CC - 1 else None),
                                   waits=([("G", rope_d), ("V", b.now("V"))] if c == 0 else []))
                        b.emit("v", lambda v,
                               o=v_sb[:, gt * VW:(gt + 1) * VW].rearrange("p (g w) -> p g w", g=NG)[:, :, 0:64],
                               i=ps_v[:, 0:256].rearrange("p (g w) -> p g w", g=NG):
                               v.tensor_copy(o, i),
                               inc="V", waits=[("T", None), ("G", VONE_G)])
                  return emit_proj
                if pending_proj is not None:
                    pending_proj()
                pending_proj = mk_proj()
                ud["T"] = b.now("T")
                ud["Sc"] = b.now("Sc")
                prev_s2b = s2_d
                scv_prev = scv
                outb_prev = outb_dg
            pending_proj()
            b.emit("y", lambda y: y.dma_start(out=k_sb[64:128, :], in_=k_sb[0:64, :]),
                   inc="DS", k=16, waits=[("Sc", None)])
            if expose_scratch >= 2:
                b.emit("y", lambda y: y.dma_start(out=qdbg, in_=q_sb[:]),
                       inc="DS", k=16, waits=[("Sc", None), ("V", None), ("DS", None)])
                b.emit("y", lambda y: y.dma_start(out=kdbg, in_=k_sb[:]),
                       inc="DS", k=16)
                b.emit("y", lambda y: y.dma_start(out=vdbg, in_=v_sb[:]),
                       inc="DS", k=16)
        p1c = b.snap()

    if 2 in phases:
        with ExitStack() as s2:
            SB2 = lambda n, sh, d2: s2.enter_context(nc.sbuf_tensor(n, sh, d2))
            PS2 = lambda n, sh: s2.enter_context(nc.psum_tensor(n, sh, F32))
            wo_sb = SB2("wo_sb", [128, CC * D], BF16)
            p_sb = [SB2(f"p_sb{i}", [128, 1024], BF16) for i in range(3)]
            dmask = [SB2(f"dmask{i}", [128, 1024], BF16) for i in range(4)]
            rec_row = SB2("rec_row", [1, CC * 1024], F32)
            o_qb = SB2("o_qb", [128, CC * 512], BF16)
            at_sb = SB2("at_sb", [128, CC * 512], F32)
            s_ps = [PS2(f"s_ps{i}", [128, 1024]) for i in range(3)]
            o_ps = PS2("o_ps", [65, 1024])
            ps_bc = s_ps[0]

            if p1c is not None:
                pw = [("T", p1c["T"]), ("Sc", p1c["Sc"]), ("V", p1c["V"]),
                      ("G", p1c["G"])]
                b.emit("g", lambda g: g.nop(), waits=pw)
                b.emit("y", lambda y: y.nop(), waits=pw)
            b.emit("g", lambda g: g.dma_start(
                out=wo_sb[:].rearrange("p (c d) -> p c d", d=D),
                in_=woTp.rearrange("(c p) d -> p c d", p=128)), inc="DG", k=16)
            WO_DG = b.now("DG")
            for j in range(4):
                b.emit("g", lambda g, m=dmask[j][:]: g.memset(m, 1.0), inc="G")
                b.emit("g", lambda g, m=dmask[j][:], base=-128 * j:
                       g.affine_select(out=m, in_=m, pattern=[[0, 2], [1, 512]],
                                       compare_op=ALU.is_ge, fill=0.0,
                                       base=base, channel_multiplier=-1),
                       inc="G", waits=[("G", None)])
            DMASK_G = b.now("G")

            if p1c is None:
                p1c = dict(T=0, Sc=0, V=0, DS=0)
            p1T, p1Sc, p1V, p1DS = p1c["T"], p1c["Sc"], p1c["V"], p1c["DS"]
            oq_ev = b.now("V")
            wo_ev = b.now("V")
            at_dma = b.now("DG")
            exp_done = [0, 0, 0]
            av_T = [0, 0, 0]
            bcfree = [("V", 0), ("V", 0)]
            epi = {"ln": 0}
            for qb in range(8):
                q0 = qb * 512
                nkv = 4 * qb + 4
                for g in range(NG):

                    for hp in range(2):
                        h0 = 4 * g + 2 * hp
                        chunk = 2 * g + hp
                        p_ready = [("Sc", 0), ("Sc", 0), ("Sc", 0)]

                        def emit_S(kv, g=g, chunk=chunk, q0=q0, qb=qb):
                            kpar = kv % 3
                            sp = s_ps[kpar]
                            pb = p_sb[kpar]
                            diag = (kv * 128 >= q0)
                            for hh in range(2):
                                w0 = []
                                if hh == 0:
                                    w0 = [("T", p1T), ("Sc", max(p1Sc, exp_done[kpar])),
                                          ("V", wo_ev), ("DS", p1DS)]
                                b.emit("t", lambda t, o=sp[:, hh * 512:(hh + 1) * 512],
                                       l=k_sb[64 * hh:64 * hh + 64, g * S + kv * 128: g * S + (kv + 1) * 128],
                                       r=q_sb[64 * hh:64 * hh + 64, chunk * S + q0: chunk * S + q0 + 512]:
                                       t.matmul(o, l, r, start=True, stop=True),
                                       inc=("T" if hh == 1 else None), waits=w0)
                            s_T = b.now("T")
                            b.emit("s", lambda s, o=pb[:, :], i=sp[:, :]:
                                   s.activation(out=o, in_=i, func=AF.Exp, scale=0.125),
                                   inc="Sc",
                                   waits=[("T", s_T), ("T", av_T[kpar])])
                            exp_done[kpar] = b.now("Sc")
                            if diag:
                                b.emit("v", lambda v, o=pb[:, :], m=dmask[kv - 4 * qb][:]:
                                       v.tensor_mul(o, o, m),
                                       inc="V", waits=[("Sc", None), ("G", DMASK_G)])
                                p_ready[kpar] = ("V", b.now("V"))
                            else:
                                p_ready[kpar] = ("Sc", exp_done[kpar])

                        def emit_AV(kv, g=g, nkv=nkv):
                            kpar = kv % 3
                            pb = p_sb[kpar]
                            for hh in range(2):
                                b.emit("t", lambda t, o=o_ps[:, hh * 512:(hh + 1) * 512],
                                       l=v_sb[:, kv * VW + g * 65: kv * VW + g * 65 + 65],
                                       r=pb[:, hh * 512:(hh + 1) * 512],
                                       st=(kv == 0), sp2=(kv == nkv - 1):
                                       t.matmul(o, l, r, start=st, stop=sp2),
                                       inc=("T" if hh == 1 else None),
                                       waits=([p_ready[kpar], ("V", oq_ev),
                                               ("Sc", epi["ln"])]
                                              if hh == 0 else []))
                            av_T[kpar] = b.now("T")

                        for kv in range(nkv):
                            emit_S(kv)
                            if kv >= 2:
                                emit_AV(kv - 2)
                        emit_AV(nkv - 2)
                        emit_AV(nkv - 1)
                        b.emit("s", lambda s, o=rec_row[0:1, chunk * 1024:(chunk + 1) * 1024],
                               i=o_ps[64:65, 0:1024]:
                               s.activation(out=o, in_=i, func=AF.Ln),
                               inc="Sc", waits=[("T", None)])
                        epi["ln"] = b.now("Sc")
                        b.emit("s", lambda s, o=rec_row[0:1, chunk * 1024:(chunk + 1) * 1024],
                               i=rec_row[0:1, chunk * 1024:(chunk + 1) * 1024]:
                               s.activation(out=o, in_=i, func=AF.Exp, scale=-1.0),
                               inc="Sc", waits=[("Sc", None)])
                        rec_sc = b.now("Sc")
                        for hh in range(2):
                            b.emit("v", lambda v,
                                   o=o_qb[64 * hh:64 * hh + 64, chunk * 512:(chunk + 1) * 512],
                                   a=o_ps[0:64, hh * 512:(hh + 1) * 512]:
                                   v.tensor_copy(o, a), inc="V", waits=[("T", None)])
                        oq_ev = b.now("V")
                for cc in range(CC):
                    parb = cc % 2
                    for hh in range(2):
                        b.emit("t", lambda t,
                               o=ps_bc[64 * hh:64 * hh + 64, parb * 512:(parb + 1) * 512],
                               l=ones_r[0:1, 0:64],
                               r=rec_row[0:1, cc * 1024 + hh * 512: cc * 1024 + (hh + 1) * 512]:
                               t.matmul(o, l, r, start=True, stop=True,
                                        skip_group_check=True),
                               inc=("T" if hh == 1 else None),
                               waits=([("Sc", rec_sc), ("Sc", exp_done[0]),
                                       bcfree[parb]] if hh == 0 else []))
                    b.emit("v", lambda v, o=o_qb[:, cc * 512:(cc + 1) * 512],
                           i=o_qb[:, cc * 512:(cc + 1) * 512],
                           m=ps_bc[:, parb * 512:(parb + 1) * 512]:
                           v.tensor_mul(o, i, m),
                           inc="V", waits=[("T", None), ("V", None)])
                    bcfree[parb] = ("V", b.now("V"))
                wo_ev = b.now("V")
                oq_all = b.now("V")
                wo_prev = [("Sc", exp_done[0]), ("Sc", exp_done[0])]
                for c in range(CC):
                    half = c % 2
                    osl = slice(half * 512, (half + 1) * 512)
                    for cin in range(CC):
                        w0 = ([("V", oq_all), ("DG", WO_DG), wo_prev[half]]
                              if cin == 0 else [])
                        b.emit("t", lambda t, o=s_ps[0][:, osl],
                               l=wo_sb[:, cin * D + c * 128: cin * D + (c + 1) * 128],
                               r=o_qb[:, cin * 512:(cin + 1) * 512],
                               st=(cin == 0), sp=(cin == CC - 1):
                               t.matmul(o, l, r, start=st, stop=sp),
                               inc=("T" if cin == CC - 1 else None), waits=w0)
                    b.emit("v", lambda v, o=at_sb[:, c * 512:(c + 1) * 512], i=s_ps[0][:, osl]:
                           v.tensor_copy(o, i),
                           inc="V", waits=[("T", None), ("DG", at_dma)])
                    wo_prev[half] = ("V", b.now("V"))
                    wo_ev = b.now("V")
                b.emit("g", lambda g2, o=out_r[:, :, slice(q0, q0 + 512)],
                       i=at_sb[:].rearrange("p (c t) -> p c t", t=512):
                       g2.dma_start(out=o, in_=i, accum_op=ALU.add),
                       inc="DG", k=16, waits=[("V", wo_ev), ("DS", p1DS)])
                at_dma = b.now("DG")
        p2c = b.snap()
    s12.close()

    if 3 in phases:
        CAPB = E * CAP // 128
        NF = FH // 128
        BIG = 1 << 20
        with ExitStack() as s3:
            SB3 = lambda n, sh, d2: s3.enter_context(nc.sbuf_tensor(n, sh, d2))
            PS3 = lambda n, sh: s3.enter_context(nc.psum_tensor(n, sh, F32))
            prevT = p2c["T"] if p2c else (p1c["T"] if p1c else 0)
            p1DS = p1c["DS"] if p1c else 0
            pw = [("T", prevT), ("Sc", b.now("Sc")), ("V", b.now("V")),
                  ("G", b.now("G"))]
            b.emit("g", lambda g: g.nop(), waits=pw)
            b.emit("y", lambda y: y.nop(), waits=pw)

            didx = SB3("didx", [128, 32], mybir.dt.int32)
            alt = SB3("alt", [128, 32], F32)
            breg = {}

            def _mkreg(g):
                breg["r"] = g.to_reg(E * CAP - 1)
                return g.nop()
            b.emit("g", _mkreg)

            with ExitStack() as sp3:
                SBp = lambda n, sh, d2: sp3.enter_context(nc.sbuf_tensor(n, sh, d2))
                amr = SBp("amr", [E, S], F32)
                ohb = SBp("ohb", [E, S], F32)
                rnk = SBp("rnk", [E, S], F32)
                amz = SBp("amz", [E, S], F32)
                drow = SBp("drow", [1, S], F32)
                alr = SBp("alr", [1, S], F32)
                ecol = SBp("ecol", [E, 1], F32)
                xf_sb = [SBp(f"xf_sb{i}", [128, 1024], BF16) for i in range(2)]
                xt_sb = SBp("xt_sb", [128, 1024], BF16)
                ps_pro = sp3.enter_context(nc.psum_tensor("ps_pro", [128, 512], F32))
                ps_x = sp3.enter_context(nc.psum_tensor("ps_x", [128, 1024], F32))

                b.emit("y", lambda y: y.dma_start(out=amr[:], in_=am_d),
                       inc="DS", k=16, waits=[("DS", p1DS)])
                am_ld = b.now("DS")
                b.emit("g", lambda g: g.iota(ecol[:], pattern=[[0, 1]], base=0,
                                             channel_multiplier=CAP,
                                             allow_small_or_imprecise_dtypes=True),
                       inc="G")
                b.emit("g", lambda g: g.memset(amz[:], 0.0), inc="G")
                pro_g = b.now("G")
                b.emit("v", lambda v: v.tensor_scalar(
                    out=ohb[:], in0=amr[:], scalar1=0.0, scalar2=None, op0=ALU.is_gt),
                    inc="V", waits=[("DS", am_ld)])
                b.emit("v", lambda v: v.tensor_tensor_scan(
                    rnk[:], ohb[:], amz[:], 0.0, ALU.add, ALU.add),
                    inc="V", waits=[("V", None), ("G", pro_g)])
                b.emit("v", lambda v: v.tensor_sub(rnk[:], rnk[:], ohb[:]),
                       inc="V", waits=[("V", None)])
                b.emit("v", lambda v: v.tensor_scalar(
                    out=amz[:], in0=rnk[:], scalar1=float(CAP), scalar2=None,
                    op0=ALU.is_ge), inc="V", waits=[("V", None)])
                b.emit("v", lambda v: v.tensor_scalar(
                    out=rnk[:], in0=rnk[:], scalar1=ecol[:, 0:1], scalar2=None,
                    op0=ALU.add), inc="V", waits=[("V", None)])
                b.emit("v", lambda v: v.scalar_tensor_tensor(
                    out=rnk[:], in0=amz[:], scalar=float(BIG), in1=rnk[:],
                    op0=ALU.mult, op1=ALU.add), inc="V", waits=[("V", None)])
                b.emit("v", lambda v: v.tensor_mul(ohb[:], ohb[:], rnk[:]),
                       inc="V", waits=[("V", None)])
                for srcb, dst in ((ohb, drow), (amr, alr)):
                    for c in range(CC):
                        cs2 = slice(c * 512, (c + 1) * 512)
                        b.emit("t", lambda t, o=ps_pro[0:1, :], l=ones_f[0:E, 0:1],
                               r=srcb[:, cs2]:
                               t.matmul(o, l, r, start=True, stop=True),
                               inc="T", waits=[("V", None)])
                        b.emit("v", lambda v, o=dst[0:1, cs2], i=ps_pro[0:1, :]:
                               v.tensor_copy(o, i), inc="V", waits=[("T", None)])
                for tt in range(32):
                    b.emit("t", lambda t, o=ps_x[:, tt:tt + 1],
                           l=drow[0:1, tt * 128:(tt + 1) * 128], r=ones_f[0:1, 0:1],
                           st=(tt == 0), sp=(tt == 31):
                           t.matmul(o, l, r, start=st, stop=sp),
                           inc=("T" if tt == 31 else None),
                           waits=([("V", None)] if tt == 0 else []))
                for tt in range(32):
                    b.emit("t", lambda t, o=ps_x[:, 512 + tt:512 + tt + 1],
                           l=alr[0:1, tt * 128:(tt + 1) * 128], r=ones_f[0:1, 0:1],
                           st=(tt == 0), sp=(tt == 31):
                           t.matmul(o, l, r, start=st, stop=sp),
                           inc=("T" if tt == 31 else None), waits=[])
                b.emit("v", lambda v: v.tensor_copy(didx[:], ps_x[:, 0:32]),
                       inc="V", waits=[("T", None)])
                b.emit("v", lambda v: v.tensor_copy(alt[:], ps_x[:, 512:544]),
                       inc="V", waits=[("T", None)])
                idx_v = b.now("V")
                b.emit("g", lambda g: g.memset(xt_sb[:], 0.0), inc="G")
                zg = b.now("G")
                for r in range(CAPB):
                    b.emit("g", lambda g, o=xs_d[r * 128:(r + 1) * 128, :]:
                           g.dma_start(out=o, in_=xt_sb[:]),
                           inc="DG", k=16, waits=([("G", zg)] if r == 0 else []))
                zs_dg = b.now("DG")
                xf_users = [0, 0]
                for tt in range(32):
                    par = tt % 2
                    b.emit("y", lambda y, o=xf_sb[par][:].rearrange("p (c t) -> p c t", t=128),
                           i=xhat_r[:, :, slice(tt * 128, (tt + 1) * 128)]:
                           y.dma_start(out=o, in_=i),
                           inc="DS", k=16, waits=[("T", xf_users[par]), ("DS", p1DS)])
                    xf_ld = b.now("DS")
                    for cc in range(CC):
                        half = cc // 4
                        b.emit("t", lambda t,
                               o=ps_x[:, half * 512 + (cc % 4) * 128: half * 512 + (cc % 4 + 1) * 128],
                               l=xf_sb[par][:, cc * 128:(cc + 1) * 128], r=ident_bf[:],
                               st=(cc % 4 == 0), sp=(cc % 4 == 3):
                               t.matmul(o, l, r, start=st, stop=sp),
                               inc=("T" if cc % 4 == 3 else None),
                               waits=([("DS", xf_ld), ("V", None), ("V", CONST_V)]
                                      if cc == 0 else []))
                    xf_users[par] = b.now("T")
                    b.emit("v", lambda v: v.tensor_copy(xt_sb[:], ps_x[:, 0:1024]),
                           inc="V", waits=[("T", None), ("DG", None)])
                    b.emit("g", lambda g, ix=didx[:, tt:tt + 1]:
                           g.indirect_dma_start(
                               out=xs_d, out_offset=bass.IndirectOffsetOnAxis(ix, 0),
                               in_=xt_sb[:], in_offset=None,
                               bounds_check=breg["r"], oob_is_err=False),
                           inc="DG", k=16,
                           waits=[("V", None), ("DG", zs_dg), ("V", idx_v)])
                scat_dg = b.now("DG")

            sl = ExitStack()
            SB3 = lambda n, sh, d2: sl.enter_context(nc.sbuf_tensor(n, sh, d2))
            w1s = [SB3(f"w1s{i}", [128, CC * FH], BF16) for i in range(2)]
            w2s = [SB3(f"w2s{i}", [128, NF * D], BF16) for i in range(2)]
            term_acc = SB3("term_acc", [128, CC * CAP], F32)
            h_sb = SB3("h_sb", [128, CC * CAP], BF16)
            xst = SB3("xst", [128, 6 * 1024], BF16)
            a_sb = SB3("a_sb", [128, NF * 512], BF16)
            ttm = [SB3(f"ttm{i}", [128, 1024], BF16) for i in range(2)]
            g_sb = SB3("g_sb", [128, CC], F32)
            be_sb = SB3("be_sb", [128, CC], F32)
            b1e = SB3("b1e", [128, F // 128], F32)
            a_ps = PS3("a_ps", [128, 1024])
            y_ps = PS3("y_ps", [128, 2048])

            def load_unit(u, wpar, wait_T):
                e1, fh1 = u // 2, u % 2
                for c in range(CC):
                    b.emit("g", lambda g2, o=w1s[wpar][:, c * FH:(c + 1) * FH],
                           i=w1d[e1, c * 128:(c + 1) * 128, fh1 * FH:(fh1 + 1) * FH]:
                           g2.dma_start(out=o, in_=i),
                           inc="DG", k=16, waits=([("T", wait_T)] if c == 0 else []))
                for f in range(NF):
                    b.emit("g", lambda g2, o=w2s[wpar][:, f * D:(f + 1) * D],
                           i=w2d[e1, fh1 * FH + f * 128: fh1 * FH + (f + 1) * 128, 0:D]:
                           g2.dma_start(out=o, in_=i), inc="DG", k=16)
                return b.now("DG")

            wready = [0, 0]
            wready[0] = load_unit(0, 0, 0)
            unit_endT = {}
            ev_prev = {0: ("Sc", 0), 1: ("V", 0)}
            y_bank_free = [("V", 0)] * 4
            xst_state = {"users": 0}
            tacc_tr_T = 0
            term_st = b.now("DS")
            TBL = ((0, 512), (512, 256))
            hnext = {"h_v": 0, "vec_ds": 0}

            def emit_xst_h(e2, prev_hv):
                b.emit("y", lambda y, o=g_sb[:, :], i=gamT[e2]: y.dma_start(out=o, in_=i),
                       inc="DS", k=16, waits=[("G", None), ("V", prev_hv)])
                b.emit("y", lambda y, o=be_sb[:, :], i=betT[e2]: y.dma_start(out=o, in_=i),
                       inc="DS", k=16)
                b.emit("y", lambda y, o=b1e[:, :], i=b1T[e2]: y.dma_start(out=o, in_=i),
                       inc="DS", k=16, waits=[("Sc", None), ("V", None)])
                vec_ds = b.now("DS")
                b.emit("y", lambda y, o=xst[:].rearrange("p (a c) -> p a c", c=D),
                       i=xs_d[e2 * CAP:(e2 + 1) * CAP, :].rearrange("(a p) c -> p a c", p=128):
                       y.dma_start(out=o, in_=i),
                       inc="DS", k=16, waits=[("T", xst_state["users"]), ("DG", scat_dg)])
                xst_ld = b.now("DS")
                h_war_T = b.now("T")
                for cc in range(CC):
                    for grp in range(2):
                        half = (2 * cc + grp) % 2
                        for tq3 in range(3):
                            tq = grp * 3 + tq3
                            b.emit("t", lambda t,
                                   o=a_ps[:, half * 512 + tq3 * 128: half * 512 + (tq3 + 1) * 128],
                                   l=xst[:, tq * 1024 + cc * 128: tq * 1024 + (cc + 1) * 128],
                                   r=ident_bf[:],
                                   st=(tq3 == 0), sp=(tq3 == 2):
                                   t.matmul(o, l, r, start=st, stop=sp),
                                   inc=("T" if tq3 == 2 else None),
                                   waits=([("DS", xst_ld), ev_prev[half], ("V", CONST_V)]
                                          if tq3 == 0 else []))
                        b.emit("v", lambda v,
                               o=h_sb[:, cc * CAP + grp * 384: cc * CAP + (grp + 1) * 384],
                               i=a_ps[:, half * 512: half * 512 + 384],
                               sc=g_sb[:, cc:cc + 1], bi=be_sb[:, cc:cc + 1]:
                               v.tensor_scalar(out=o, in0=i, scalar1=sc, scalar2=bi,
                                               op0=ALU.mult, op1=ALU.add),
                               inc="V", waits=[("T", None), ("DS", vec_ds),
                                               ("T", h_war_T)])
                        ev_prev[half] = ("V", b.now("V"))
                xst_state["users"] = b.now("T")
                return {"h_v": b.now("V"), "vec_ds": vec_ds}

            hnext = emit_xst_h(0, 0)
            for u in range(NU):
                e, fh = u // 2, u % 2
                wpar = u % 2
                if u + 1 < NU:
                    wready[(u + 1) % 2] = load_unit(u + 1, (u + 1) % 2,
                                                    unit_endT.get(u - 1, 0))
                if fh == 0:
                    hcur = hnext
                for i_tb, (off, tbn) in enumerate(TBL):
                    for f in range(NF):
                        fp = f % 2
                        asl = slice(fp * 512, fp * 512 + tbn)
                        for c in range(CC):
                            w0 = []
                            if c == 0:
                                w0 = [("V", hcur["h_v"]), ("DG", wready[wpar]), ev_prev[fp]]
                            b.emit("t", lambda t, o=a_ps[:, asl],
                                   l=w1s[wpar][:, c * FH + f * 128: c * FH + (f + 1) * 128],
                                   r=h_sb[:, c * CAP + off: c * CAP + off + tbn],
                                   st=(c == 0), sp=(c == CC - 1):
                                   t.matmul(o, l, r, start=st, stop=sp),
                                   inc=("T" if c == CC - 1 else None), waits=w0)
                        aT = b.now("T")
                        if f % 2 == 0:
                            b.emit("s", lambda s, o=a_sb[:, f * 512: f * 512 + tbn],
                                   i=a_ps[:, asl],
                                   bi=b1e[:, fh * NF + f: fh * NF + f + 1]:
                                   s.activation(out=o, in_=i, func=AF.Relu, bias=bi),
                                   inc="Sc", waits=[("T", aT), ("DS", hcur["vec_ds"])])
                            ev_prev[fp] = ("Sc", b.now("Sc"))
                        else:
                            b.emit("v", lambda v, o=a_sb[:, f * 512: f * 512 + tbn],
                                   i=a_ps[:, asl],
                                   bi=b1e[:, fh * NF + f: fh * NF + f + 1]:
                                   v.tensor_scalar(out=o, in0=i, scalar1=bi, scalar2=0.0,
                                                   op0=ALU.add, op1=ALU.max),
                                   inc="V", waits=[("T", aT), ("DS", hcur["vec_ds"])])
                            ev_prev[fp] = ("V", b.now("V"))
                        ev_sc = b.now("Sc")
                        ev_v = b.now("V")
                    if fh == 1 and i_tb == len(TBL) - 1 and e + 1 < E:
                        hnext = emit_xst_h(e + 1, hcur["h_v"])
                    for cc2 in range(CC):
                        bank = cc2 % 4
                        for f in range(NF):
                            w0 = []
                            if f == 0:
                                w0 = [("Sc", ev_sc), ("V", ev_v), y_bank_free[bank]]
                            b.emit("t", lambda t, o=y_ps[:, bank * 512: bank * 512 + tbn],
                                   l=w2s[wpar][:, f * D + cc2 * 128: f * D + (cc2 + 1) * 128],
                                   r=a_sb[:, f * 512: f * 512 + tbn],
                                   st=(f == 0), sp=(f == NF - 1):
                                   t.matmul(o, l, r, start=st, stop=sp),
                                   inc=("T" if f == NF - 1 else None), waits=w0)
                        yT = b.now("T")
                        if fh == 0:
                            b.emit("v", lambda v,
                                   o=term_acc[:, cc2 * CAP + off: cc2 * CAP + off + tbn],
                                   i=y_ps[:, bank * 512: bank * 512 + tbn]:
                                   v.tensor_copy(o, i),
                                   inc="V", waits=[("T", yT), ("T", tacc_tr_T)])
                        else:
                            b.emit("v", lambda v,
                                   o=term_acc[:, cc2 * CAP + off: cc2 * CAP + off + tbn],
                                   a=term_acc[:, cc2 * CAP + off: cc2 * CAP + off + tbn],
                                   i=y_ps[:, bank * 512: bank * 512 + tbn]:
                                   v.tensor_add(o, a, i),
                                   inc="V", waits=[("T", yT), ("V", None)])
                        y_bank_free[bank] = ("V", b.now("V"))
                if fh == 1:
                    for tq in range(6):
                        tpar = tq % 2
                        for grp in range(2):
                            half = grp
                            for c4 in range(4):
                                cc = grp * 4 + c4
                                b.emit("t", lambda t,
                                       o=a_ps[:, half * 512 + c4 * 128: half * 512 + (c4 + 1) * 128],
                                       l=term_acc[:, cc * CAP + tq * 128: cc * CAP + tq * 128 + 128],
                                       r=ident[:],
                                       st=(c4 == 0), sp=(c4 == 3):
                                       t.matmul(o, l, r, start=st, stop=sp),
                                       inc=("T" if c4 == 3 else None),
                                       waits=([("V", None), ev_prev[half]]
                                              if c4 == 0 else []))
                            b.emit("v", lambda v, o=ttm[tpar][:, half * 512:(half + 1) * 512],
                                   i=a_ps[:, half * 512:(half + 1) * 512]:
                                   v.tensor_copy(o, i),
                                   inc="V", waits=[("T", None), ("DS", term_st)])
                            ev_prev[half] = ("V", b.now("V"))
                        b.emit("y", lambda y, o=term_d[e * CAP + tq * 128: e * CAP + (tq + 1) * 128, :],
                               i=ttm[tpar][:]: y.dma_start(out=o, in_=i),
                               inc="DS", k=16, waits=[("V", None)])
                        term_st = b.now("DS")
                    tacc_tr_T = b.now("T")
                    h_users_T = b.now("T")
                unit_endT[u] = b.now("T")

            sl.close()
            uw = [("T", None), ("Sc", None), ("V", None), ("DS", None),
                  ("DG", None)]
            b.emit("g", lambda g: g.nop(), waits=uw)
            b.emit("v", lambda v: v.nop(), waits=uw)
            b.emit("y", lambda y: y.nop(), waits=uw)
            with ExitStack() as su:
                SBu = lambda n, sh, d2: su.enter_context(nc.sbuf_tensor(n, sh, d2))
                gt_sb = [SBu(f"gt_sb{i}", [128, 1024], BF16) for i in range(2)]
                ff_sb = [SBu(f"ff_sb{i}", [128, 1024], F32) for i in range(2)]
                am2 = SBu("am2", [E, T3], F32)
                y_read = [0, 0]
                ff_dma = [b.now("DG")] * 2
                gt_mul = [0, 0]
                for tt in range(32):
                    up = tt % 2
                    b.emit("g", lambda g, o=gt_sb[up][:]: g.memset(o, 0.0),
                           inc="G", waits=[("V", gt_mul[up]), ("T", y_read[up])])
                    gt_g = b.now("G")
                    b.emit("g", lambda g, ix=didx[:, tt:tt + 1], o=gt_sb[up][:]:
                           g.indirect_dma_start(
                               out=o, out_offset=None,
                               in_=term_d, in_offset=bass.IndirectOffsetOnAxis(ix, 0),
                               bounds_check=breg["r"], oob_is_err=False),
                           inc="DG", k=16,
                           waits=[("DS", term_st), ("G", gt_g)])
                    gt_dg = b.now("DG")
                    b.emit("v", lambda v, m=alt[:, tt:tt + 1], o=gt_sb[up][:]:
                           v.tensor_scalar_mul(o, o, m),
                           inc="V", waits=[("DG", gt_dg), ("T", y_read[up])])
                    gf_v = b.now("V")
                    gt_mul[up] = gf_v
                    for cc in range(CC):
                        half = cc // 4
                        bank2 = up * 2 + half
                        b.emit("t", lambda t,
                               o=y_ps[:, bank2 * 512 + (cc % 4) * 128: bank2 * 512 + (cc % 4 + 1) * 128],
                               l=gt_sb[up][:, cc * 128:(cc + 1) * 128], r=ident_bf[:],
                               st=(cc % 4 == 0), sp=(cc % 4 == 3):
                               t.matmul(o, l, r, start=st, stop=sp),
                               inc=("T" if cc % 4 == 3 else None),
                               waits=([("V", gf_v), y_bank_free[up * 2],
                                       y_bank_free[up * 2 + 1]]
                                      if cc == 0 else []))
                    y_read[up] = b.now("T")
                    b.emit("v", lambda v, o=ff_sb[up][:],
                           i=y_ps[:, up * 1024:(up + 1) * 1024]:
                           v.tensor_copy(o, i),
                           inc="V", waits=[("T", None), ("DG", ff_dma[up])])
                    y_bank_free[up * 2] = ("V", b.now("V"))
                    y_bank_free[up * 2 + 1] = ("V", b.now("V"))
                    b.emit("g", lambda g2, o=out_r[:, :, slice(tt * 128, (tt + 1) * 128)],
                           i=ff_sb[up][:].rearrange("p (c t) -> p c t", t=128):
                           g2.dma_start(out=o, in_=i, accum_op=ALU.add),
                           inc="DG", k=16, waits=[("V", None)])
                    ff_dma[up] = b.now("DG")
                bt_dma = [b.now("DG")] * 2
                bt_ev = [("V", b.now("V")), ("V", b.now("V"))]
                for tb in range(NT3):
                    tsl = slice(tb * T3, (tb + 1) * T3)
                    b.emit("y", lambda y, o=am2[:, :], i=am_d[0:E, tsl]:
                           y.dma_start(out=o, in_=i),
                           inc="DS", k=16, waits=[("T", None), ("V", None)])
                    am2_ld = b.now("DS")
                    for c in range(CC):
                        bb2 = c % 2
                        b.emit("t", lambda t, o=a_ps[:, bb2 * 512:bb2 * 512 + T3],
                               l=b2_sb[0:E, c * 128:(c + 1) * 128],
                               r=am2[:, :]: t.matmul(o, l, r, start=True, stop=True),
                               inc="T",
                               waits=[("DS", max(am2_ld, CONST_DS)), bt_ev[bb2]])
                        b.emit("v", lambda v, o=ff_sb[c % 2][:, 0:T3],
                               i=a_ps[:, bb2 * 512:bb2 * 512 + T3]:
                               v.tensor_copy(o, i), inc="V",
                               waits=[("T", None), ("DG", bt_dma[c % 2])])
                        bt_ev[bb2] = ("V", b.now("V"))
                        b.emit("g", lambda g2,
                               o=out_r[:, c:c + 1, tsl],
                               i=ff_sb[c % 2][:, 0:T3].rearrange("p (o t) -> p o t", o=1):
                               g2.dma_start(out=o, in_=i, accum_op=ALU.add),
                               inc="DG", k=16, waits=[("V", None)])
                        bt_dma[c % 2] = b.now("DG")

    b.emit("g", lambda g2: g2.nop(), waits=[("DG", None), ("DS", None)])
    b.emit("y", lambda y: y.nop(), waits=[("DG", None), ("DS", None)])

    b.flush(blk)
    es.close()
    return nc


_PERM = np.concatenate([np.arange(0, D, 2), np.arange(1, D, 2)])


def host_inputs(L0_attn_X, L0_attn_W_q, L0_attn_W_k, L0_attn_W_v, L0_attn_W_o,
                L0_ffn_centroids, L0_ffn_gamma, L0_ffn_beta, L0_ffn_W1,
                L0_ffn_b1, L0_ffn_W2, L0_ffn_b2):
    f32 = np.float32
    perm = _PERM
    X = np.asarray(L0_attn_X, f32)
    theta = (1.0 / 10000.0 ** (np.arange(D // 2, dtype=np.float64) / (D // 2)))
    ang = np.arange(S, dtype=np.float64)[:, None] * theta[None, :]
    cosT = np.ascontiguousarray(np.cos(ang).T.astype(f32))
    sinT = np.ascontiguousarray(np.sin(ang).T.astype(f32))

    Wq = np.asarray(L0_attn_W_q, f32)
    Wk = np.asarray(L0_attn_W_k, f32)
    Wv = np.asarray(L0_attn_W_v, f32)
    Wo = np.asarray(L0_attn_W_o, f32)
    cen = np.asarray(L0_ffn_centroids, f32)
    gam = np.asarray(L0_ffn_gamma, f32)
    bet = np.asarray(L0_ffn_beta, f32)
    W1 = np.asarray(L0_ffn_W1, f32)
    b1 = np.asarray(L0_ffn_b1, f32)
    W2 = np.asarray(L0_ffn_W2, f32)
    b2 = np.asarray(L0_ffn_b2, f32)

    shared = {
        "wqT": np.ascontiguousarray(Wq.T),
        "wkT": np.ascontiguousarray(Wk.T),
        "wvT": np.ascontiguousarray(Wv.T),
        "woTp": np.ascontiguousarray(Wo.T[:, perm]),
        "cenTp": np.ascontiguousarray(cen.T[perm]),
        "gamT": np.ascontiguousarray(gam[:, perm].reshape(E, CC, 128).transpose(0, 2, 1)),
        "betT": np.ascontiguousarray(bet[:, perm].reshape(E, CC, 128).transpose(0, 2, 1)),
        "b1T": np.ascontiguousarray(b1.reshape(E, F // 128, 128).transpose(0, 2, 1)),
        "b2fm": np.ascontiguousarray(b2[:, perm]),
        "w1d": np.ascontiguousarray(W1.transpose(0, 2, 1)[:, perm, :]),
        "w2d": np.ascontiguousarray(W2.transpose(0, 2, 1)[:, :, perm]),
        "cosT": cosT,
        "sinT": sinT,
    }
    in_maps = []
    for bb in range(NC):
        m = dict(shared)
        m["xTp"] = np.ascontiguousarray(X[bb].T[perm])
        in_maps.append(m)
    return in_maps


def kernel(**inputs):
    in_maps = host_inputs(**inputs)
    nc = build_program()
    res = run_bass_kernel_spmd(nc, in_maps, core_ids=list(range(NC)))
    iperm = np.argsort(_PERM)
    outs = [np.asarray(r["out"])[iperm].T for r in res.results]
    return np.stack(outs).astype(np.float32)


# version 14 (final)
# speedup vs baseline: 1.0070x; 1.0070x over previous
import numpy as np
import ml_dtypes
from contextlib import ExitStack

import concourse.bass as bass
import concourse.mybir as mybir
from concourse.bass_utils import run_bass_kernel_spmd

F32 = mybir.dt.float32
BF16 = mybir.dt.bfloat16
AF = mybir.ActivationFunctionType
ALU = mybir.AluOpType
AX = mybir.AxisListType

D = 1024
S = 4096
NH, NG, DK = 16, 4, 64
E, F = 8, 4096
EPS = 1e-5
NC = 8
CC = D // 128
FH = F // 2
NU = 2 * E
T1 = 256
NT1 = S // T1
T3 = 512
NT3 = S // T3
VW = NG * 65

ENGS = ("t", "s", "v", "g", "y")


class B:
    def __init__(self, nc, es):
        self.nc = nc
        self.q = {k: [] for k in ENGS}
        self.sems = {}
        self.cnt = {}
        self.waited = {k: {} for k in ENGS}
        for ch in ("T", "Sc", "V", "G", "DS", "DG"):
            self.sems[ch] = es.enter_context(nc.semaphore(ch))
            self.cnt[ch] = 0

    def now(self, ch):
        return self.cnt[ch]

    def snap(self):
        return dict(self.cnt)

    def emit(self, eng, fn, inc=None, k=1, waits=()):
        rw = []
        wd = self.waited[eng]
        for ch, v in waits:
            v = self.cnt[ch] if v is None else v
            if v > 0 and wd.get(ch, 0) < v:
                wd[ch] = v
                rw.append((self.sems[ch], v))
        if inc:
            self.cnt[inc] += k
        sem = self.sems[inc] if inc else None

        def run(e, rw=rw, fn=fn, sem=sem, k=k):
            for h, v in rw:
                e.wait_ge(h, v)
            ins = fn(e)
            if sem is not None:
                ins.then_inc(sem, k)
        self.q[eng].append(run)

    def flush(self, blk):
        qs = {k: list(v) for k, v in self.q.items()}
        for k in self.q:
            self.q[k] = []

        @blk.tensor
        def _(t):
            for fn in qs["t"]:
                fn(t)

        @blk.scalar
        def _(s):
            for fn in qs["s"]:
                fn(s)

        @blk.vector
        def _(v):
            for fn in qs["v"]:
                fn(v)

        @blk.gpsimd
        def _(g):
            for fn in qs["g"]:
                fn(g)

        @blk.sync
        def _(y):
            for fn in qs["y"]:
                fn(y)


def build_program(expose_scratch=False, phases=(1, 2, 3)):
    nc = bass.Bass("TRN2", target_bir_lowering=False)
    dt = nc.dram_tensor
    xTp = dt("xTp", [D, S], F32, kind="ExternalInput").ap()
    wqT = dt("wqT", [D, D], F32, kind="ExternalInput").ap()
    wkT = dt("wkT", [D, NG * DK], F32, kind="ExternalInput").ap()
    wvT = dt("wvT", [D, NG * DK], F32, kind="ExternalInput").ap()
    woTp = dt("woTp", [D, D], F32, kind="ExternalInput").ap()
    cenTp = dt("cenTp", [D, E], F32, kind="ExternalInput").ap()
    gamT = dt("gamT", [E, 128, CC], F32, kind="ExternalInput").ap()
    betT = dt("betT", [E, 128, CC], F32, kind="ExternalInput").ap()
    b1T = dt("b1T", [E, 128, F // 128], F32, kind="ExternalInput").ap()
    b2fm = dt("b2fm", [E, D], F32, kind="ExternalInput").ap()
    w1d = dt("w1d", [E, D, F], F32, kind="ExternalInput").ap()
    w2d = dt("w2d", [E, F, D], F32, kind="ExternalInput").ap()
    cosT = dt("cosT", [D // 2, S], F32, kind="ExternalInput").ap()
    sinT = dt("sinT", [D // 2, S], F32, kind="ExternalInput").ap()
    out = dt("out", [D, S], F32, kind="ExternalOutput").ap()

    skind = "ExternalOutput" if expose_scratch else "Internal"
    xhat_d = dt("xhat_d", [D, S], BF16, kind=skind).ap()
    am_d = dt("am_d", [E, S], F32, kind=skind).ap()
    CAP = 768
    dbgkind = "ExternalOutput" if expose_scratch == 4 else "Internal"
    xs_d = dt("xs_d", [E * CAP, D], BF16, kind=dbgkind).ap()
    term_d = dt("term_d", [E * CAP, D], BF16, kind=dbgkind).ap()
    if expose_scratch >= 2:
        qdbg = dt("qdbg", [128, CC * S], BF16, kind="ExternalOutput").ap()
        kdbg = dt("kdbg", [128, NG * S], BF16, kind="ExternalOutput").ap()
        vdbg = dt("vdbg", [128, 32 * VW], BF16, kind="ExternalOutput").ap()
    if expose_scratch == 3:
        sdbg = dt("sdbg", [128, 1024], F32, kind="ExternalOutput").ap()
        pdbg = dt("pdbg", [128, 1024], BF16, kind="ExternalOutput").ap()
        odbg = dt("odbg", [65, 1024], F32, kind="ExternalOutput").ap()
    row_d = dt("row_d", [4, S], F32, kind="Internal").ap()

    out_r = out.rearrange("(c p) t -> p c t", p=128)
    xhat_r = xhat_d.rearrange("(c p) t -> p c t", p=128)
    xTp_r = xTp.rearrange("(c p) t -> p c t", p=128)
    cosT_r = cosT.rearrange("(c p) t -> p c t", p=128)
    sinT_r = sinT.rearrange("(c p) t -> p c t", p=128)

    es = ExitStack()
    b = B(nc, es)
    SB = lambda n, sh, d2: es.enter_context(nc.sbuf_tensor(n, sh, d2))
    blk = es.enter_context(nc.Block())

    ones_f = SB("ones_f", [128, 1], F32)
    ones_r = SB("ones_r", [1, 128], F32)
    ident = SB("ident", [128, 128], F32)
    cen_sb = SB("cen_sb", [128, CC * E], F32)
    b2_sb = SB("b2_sb", [E, D], F32)

    eps_sb = SB("eps_sb", [1, 1], F32)
    eps2_sb = SB("eps2_sb", [1, 1], F32)
    b.emit("g", lambda g: g.memset(eps_sb[:], EPS), inc="G")
    b.emit("g", lambda g: g.memset(eps2_sb[:], EPS * EPS), inc="G")
    b.emit("g", lambda g: g.memset(ones_f[:], 1.0), inc="G")
    b.emit("g", lambda g: g.memset(ones_r[:], 1.0), inc="G")
    b.emit("g", lambda g: g.memset(ident[:], 1.0), inc="G")
    b.emit("g", lambda g: g.affine_select(
        out=ident[:], in_=ident[:], pattern=[[-1, 128]],
        compare_op=ALU.is_equal, fill=0.0, base=0, channel_multiplier=1),
        inc="G", waits=[("G", None)])
    b.emit("y", lambda y: y.dma_start(
        out=cen_sb[:].rearrange("p (c e) -> p c e", e=E),
        in_=cenTp.rearrange("(c p) e -> p c e", p=128)), inc="DS", k=16)
    b.emit("y", lambda y: y.dma_start(out=b2_sb[:], in_=b2fm), inc="DS", k=16)
    CONST_G = b.now("G")
    ident_bf = SB("ident_bf", [128, 128], BF16)
    b.emit("v", lambda v: v.tensor_copy(ident_bf[:], ident[:]),
           inc="V", waits=[("G", CONST_G)])
    CONST_V = b.now("V")
    CONST_DS = b.now("DS")

    s12 = ExitStack()
    q_sb = s12.enter_context(nc.sbuf_tensor("q_sb", [128, CC * S], BF16))
    k_sb = s12.enter_context(nc.sbuf_tensor("k_sb", [128, NG * S], BF16))
    v_sb = s12.enter_context(nc.sbuf_tensor("v_sb", [128, 32 * VW], BF16))

    p1c = p2c = None

    if 1 in phases:
        with ExitStack() as s1:
            SB1 = lambda n, sh, d2: s1.enter_context(nc.sbuf_tensor(n, sh, d2))
            PS1 = lambda n, sh: s1.enter_context(nc.psum_tensor(n, sh, F32))
            wq_sb = SB1("wq_sb", [128, CC * D], BF16)
            wk_sb = SB1("wk_sb", [128, CC * NG * DK], BF16)
            wv_sb = SB1("wv_sb", [128, CC * NG * DK], BF16)
            x_sb = [SB1(f"x_sb{i}", [128, CC * T1], F32) for i in range(2)]
            xn32 = SB1("xn32", [128, CC * T1], F32)
            scr = SB1("scr", [128, CC * T1], F32)
            xhbf = SB1("xhbf", [128, CC * T1], BF16)
            sq_sb = SB1("sq_sb", [128, CC * T1], F32)
            xr_t2 = [SB1(f"xr_t{i}", [128, CC * T1], BF16) for i in range(2)]
            cs_sb = SB1("cs_sb", [128, 2 * T1], F32)
            rows = SB1("rows", [1, 7 * T1], F32)
            sc_sb = SB1("sc_sb", [E, T1], F32)
            sctm = SB1("sctm", [128, E], F32)
            sctm_b = SB1("sctm_b", [128, E], F32)
            onez = sctm
            onez_b = sctm_b
            rq = SB1("rq", [128, 3], F32)
            rq_b = SB1("rq_b", [128, 3], F32)
            amrow = SB1("amrow", [E, 128], F32)
            amrow_b = SB1("amrow_b", [E, 128], F32)
            ps_s12 = PS1("ps_s12", [1, 2 * T1])
            ps_s1 = ps_s12[0:1, 0:T1]
            ps_s2 = ps_s12[0:1, T1:2 * T1]
            ps_tr2 = PS1("ps_tr2", [128, 512])
            ps_sc = PS1("ps_sc", [E, T1])
            ps_bc = PS1("ps_bc1", [128, 1024])
            ps_pj = PS1("ps_pj", [128, 1024])
            ps_v = PS1("ps_v", [128, 512])
            ps_tr = ps_v

            b.emit("g", lambda g: g.dma_start(
                out=wq_sb[:].rearrange("p (c d) -> p c d", d=D),
                in_=wqT.rearrange("(c p) d -> p c d", p=128)), inc="DG", k=16)
            b.emit("g", lambda g: g.dma_start(
                out=wk_sb[:].rearrange("p (c d) -> p c d", d=NG * DK),
                in_=wkT.rearrange("(c p) d -> p c d", p=128)), inc="DG", k=16)
            b.emit("g", lambda g: g.dma_start(
                out=wv_sb[:].rearrange("p (c d) -> p c d", d=NG * DK),
                in_=wvT.rearrange("(c p) d -> p c d", p=128)), inc="DG", k=16)
            b.emit("g", lambda g: g.memset(v_sb[:], 1.0), inc="G")
            WCAST_DG = b.now("DG")
            VONE_G = b.now("G")

            R_MU, R_R, R_SM, R_MSQ, R_VAR, R_A, R_R2 = 0, 1, 2, 3, 4, 5, 6
            rsl = lambda i: (slice(0, 1), slice(i * T1, (i + 1) * T1))

            csl = [slice(c * T1, (c + 1) * T1) for c in range(CC)]
            x_users_done = [dict(V=0, T=0, Sc=0), dict(V=0, T=0, Sc=0)]
            pj_prev = [("Sc", 0), ("Sc", 0)]
            bc_users = [0, 0]
            prev_s2b = 0
            xh_dma_prev = 0
            ps_free = 0
            scv_prev = 0
            outb_prev = 0
            rope_prev = 0
            pending_proj = None

            for tb in range(NT1):
                par = tb % 2
                xs = x_sb[par]
                tsl = slice(tb * T1, (tb + 1) * T1)
                ud = x_users_done[par]
                b.emit("y", lambda y, o=xs[:].rearrange("p (c t) -> p c t", t=T1),
                       i=xTp_r[:, :, tsl]: y.dma_start(out=o, in_=i),
                       inc="DS", k=16,
                       waits=[("V", ud["V"]), ("T", ud["T"]), ("Sc", ud["Sc"])])
                ld_x = b.now("DS")

                for c in range(CC):
                    if c < 4:
                        b.emit("s", lambda s, o=sq_sb[:, csl[c]], i=xs[:, csl[c]]:
                               s.activation(out=o, in_=i, func=AF.Square),
                               inc="Sc", waits=[("DS", ld_x), ("T", prev_s2b)])
                    else:
                        b.emit("g", lambda g2, o=sq_sb[:, csl[c]], a=xs[:, csl[c]]:
                               g2.tensor_mul(o, a, a),
                               inc="G", waits=[("DS", ld_x), ("T", prev_s2b)])
                    b.emit("t", lambda t, o=ps_s1, l=ones_f[:, 0:1], r=xs[:, csl[c]],
                           st=(c == 0), sp=(c == CC - 1):
                           t.matmul(o, l, r, start=st, stop=sp),
                           inc=("T" if c == CC - 1 else None),
                           waits=[("DS", ld_x), ("V", ps_free)])
                sq1 = b.now("Sc")
                sqg = b.now("G")
                s1_d = b.now("T")
                for c in range(CC):
                    b.emit("t", lambda t, o=ps_s2, l=ones_f[:, 0:1], r=sq_sb[:, csl[c]],
                           st=(c == 0), sp=(c == CC - 1):
                           t.matmul(o, l, r, start=st, stop=sp),
                           inc=("T" if c == CC - 1 else None),
                           waits=([("Sc", sq1), ("G", sqg)] if c == 0 else []))
                s2_d = b.now("T")

                b.emit("v", lambda v, o=rows[rsl(R_MU)], i=ps_s1:
                       v.tensor_scalar_mul(o, i, 1.0 / D),
                       inc="V", waits=[("T", s2_d)])
                b.emit("v", lambda v, o=rows[rsl(R_SM)], i=ps_s2:
                       v.tensor_scalar_mul(o, i, 1.0 / D),
                       inc="V", waits=[("T", s2_d)])
                muV = b.now("V")
                ps_free = muV
                b.emit("s", lambda s, o=rows[rsl(R_MSQ)], i=rows[rsl(R_MU)]:
                       s.activation(out=o, in_=i, func=AF.Square),
                       inc="Sc", waits=[("V", muV)])
                b.emit("v", lambda v, o=rows[rsl(R_VAR)], a=rows[rsl(R_SM)], c2=rows[rsl(R_MSQ)]:
                       v.tensor_sub(o, a, c2),
                       inc="V", waits=[("Sc", None), ("V", None)])
                v1_v = b.now("V")
                b.emit("s", lambda s, o=rows[rsl(R_A)], i=rows[rsl(R_VAR)]:
                       s.activation(out=o, in_=i, func=AF.Ln, bias=eps_sb[0:1, 0:1]),
                       inc="Sc", waits=[("V", v1_v), ("G", CONST_G)])
                b.emit("s", lambda s, o=rows[rsl(R_R)], i=rows[rsl(R_A)]:
                       s.activation(out=o, in_=i, func=AF.Exp, scale=-0.5),
                       inc="Sc", waits=[("Sc", None)])
                rS = b.now("Sc")
                b.emit("s", lambda s, o=rows[rsl(R_R2)], i=rows[rsl(R_VAR)]:
                       s.activation(out=o, in_=i, func=AF.Ln, scale=1.0 + EPS,
                                    bias=eps2_sb[0:1, 0:1]),
                       inc="Sc", waits=[("V", v1_v)])
                b.emit("v", lambda v, o=rows[rsl(R_R2)], a=rows[rsl(R_A)], c2=rows[rsl(R_R2)]:
                       v.tensor_sub(o, a, c2),
                       inc="V", waits=[("Sc", None)])
                b.emit("s", lambda s, o=rows[rsl(R_R2)], i=rows[rsl(R_R2)]:
                       s.activation(out=o, in_=i, func=AF.Exp, scale=0.5),
                       inc="Sc", waits=[("V", None)])
                r2S = b.now("Sc")
                b.emit("t", lambda t, o=ps_bc[:, 0:2 * T1], l=ones_r[0:1, :],
                       r=rows[0:1, 0:2 * T1]:
                       t.matmul(o, l, r, start=True, stop=True),
                       inc="T", waits=[("V", muV), ("Sc", rS), ("V", bc_users[0]),
                                       ("G", CONST_G)])
                bc1T = b.now("T")

                for c in range(CC):
                    b.emit("v", lambda v, o=scr[:, csl[c]], a=xs[:, csl[c]], c2=ps_bc[:, 0:T1]:
                           v.tensor_sub(o, a, c2),
                           inc="V", waits=[("T", bc1T), ("DG", outb_prev),
                                           ("G", rope_prev)])
                    b.emit("v", lambda v, o=xn32[:, csl[c]], a=scr[:, csl[c]], c2=ps_bc[:, T1:2 * T1] if True else None:
                           v.tensor_mul(o, a, c2),
                           inc="V", waits=[("V", None)])
                xn_d = b.now("V")
                bc_users[0] = xn_d
                ud["V"] = xn_d

                for c in range(CC):
                    b.emit("t", lambda t, o=ps_sc[:, :], l=cen_sb[:, c * E:(c + 1) * E],
                           r=xn32[:, csl[c]], st=(c == 0), sp=(c == CC - 1):
                           t.matmul(o, l, r, start=st, stop=sp),
                           inc=("T" if c == CC - 1 else None),
                           waits=[("V", xn_d), ("DS", CONST_DS), ("V", scv_prev)])
                sc_d = b.now("T")
                b.emit("t", lambda t, o=ps_bc[:, 512:512 + T1], l=ones_r[0:1, :],
                       r=rows[rsl(R_R2)]:
                       t.matmul(o, l, r, start=True, stop=True),
                       inc="T", waits=[("Sc", r2S), ("V", bc_users[1])])
                bc2T = b.now("T")

                for c in range(CC):
                    b.emit("v", lambda v, o=xhbf[:, csl[c]], a=xn32[:, csl[c]], c2=ps_bc[:, 512:512 + T1]:
                           v.tensor_mul(o, a, c2),
                           inc="V", waits=[("T", bc2T), ("DS", xh_dma_prev)])
                xh_d = b.now("V")
                bc_users[1] = xh_d
                b.emit("y", lambda y, o=xhat_r[:, :, tsl], i=xhbf[:].rearrange("p (c t) -> p c t", t=T1):
                       y.dma_start(out=o, in_=i), inc="DS", k=16, waits=[("V", xh_d)])
                xh_dma_prev = b.now("DS")

                for c in range(CC):
                    b.emit("v", lambda v, o=scr[:, csl[c]], a=xs[:, csl[c]], c2=xn32[:, csl[c]]:
                           v.tensor_add(o, a, c2),
                           inc="V", waits=[])
                ox_d = b.now("V")
                ud["V"] = ox_d
                b.emit("g", lambda g2, o=out_r[:, :, tsl], i=scr[:].rearrange("p (c t) -> p c t", t=T1):
                       g2.dma_start(out=o, in_=i), inc="DG", k=16, waits=[("V", ox_d)])
                outb_dg = b.now("DG")

                xr_t = xr_t2[par]
                for j in range(4):
                    ja, jb = csl[j], csl[j + 4]
                    co, si = slice(0, T1), slice(T1, 2 * T1)
                    b.emit("y", lambda y, o=cs_sb[:, 0:T1], i=cosT[j * 128:(j + 1) * 128, tsl]:
                           y.dma_start(out=o, in_=i), inc="DS", k=16, waits=[("G", None)])
                    b.emit("y", lambda y, o=cs_sb[:, T1:2 * T1], i=sinT[j * 128:(j + 1) * 128, tsl]:
                           y.dma_start(out=o, in_=i), inc="DS", k=16)
                    cs_ld = b.now("DS")
                    b.emit("g", lambda g2, o=scr[:, ja], a=xn32[:, ja], c2=cs_sb[:, co]:
                           g2.tensor_mul(o, a, c2),
                           inc="G", waits=[("DS", cs_ld), ("T", None), ("DG", outb_dg),
                                           ("V", ox_d)])
                    b.emit("g", lambda g2, o=xr_t[:, ja], a=xn32[:, jb], c2=cs_sb[:, si]:
                           g2.tensor_mul(o, a, c2), inc="G")
                    b.emit("g", lambda g2, o=xr_t[:, ja], a=scr[:, ja], c2=xr_t[:, ja]:
                           g2.tensor_sub(o, a, c2), inc="G", waits=[("G", None)])
                    b.emit("g", lambda g2, o=scr[:, ja], a=xn32[:, ja], c2=cs_sb[:, si]:
                           g2.tensor_mul(o, a, c2), inc="G", waits=[("G", None)])
                    b.emit("g", lambda g2, o=xr_t[:, jb], a=xn32[:, jb], c2=cs_sb[:, co]:
                           g2.tensor_mul(o, a, c2), inc="G")
                    b.emit("g", lambda g2, o=xr_t[:, jb], a=scr[:, ja], c2=xr_t[:, jb]:
                           g2.tensor_add(o, a, c2), inc="G", waits=[("G", None)])
                rope_d = b.now("G")
                rope_prev = rope_d

                b.emit("v", lambda v, o=sc_sb[:, :], i=ps_sc[:, :]: v.tensor_copy(o, i),
                       inc="V", waits=[("T", sc_d)])
                scv = b.now("V")
                qs_ = [slice(0, 128), slice(128, 256)]
                trA = [ps_tr[:, 0:E], ps_tr2[:, 0:E]]
                trB = [ps_tr[0:E, 256:384], ps_tr2[0:E, 256:384]]
                sct = [sctm, sctm_b]
                onz = [onez, onez_b]
                rqq = [rq, rq_b]
                amr = [amrow, amrow_b]
                for q_ in range(2):
                    b.emit("t", lambda t, o=trA[q_], i=sc_sb[:, qs_[q_]], idd=ident[0:E, 0:E]:
                           t.transpose(o, i, idd),
                           inc="T", waits=[("V", None), ("G", CONST_G)])
                for q_ in range(2):
                    b.emit("v", lambda v, o=sct[q_][:, :], i=trA[q_]: v.tensor_copy(o, i),
                           inc="V", waits=[("T", None), ("V", None)])
                for q_ in range(2):
                    b.emit("v", lambda v, o=rqq[q_][:, 0:1], i=sct[q_][:, :]:
                           v.reduce_max(o, i, axis=AX.X), inc="V", waits=[("V", None)])
                mxv = b.now("V")
                for q_ in range(2):
                    b.emit("v", lambda v, o=onz[q_][:, :], i=sct[q_][:, :], m=rqq[q_][:, 0:1]:
                           v.tensor_scalar(out=o, in0=i, scalar1=m, scalar2=None, op0=ALU.is_ge),
                           inc="V", waits=[("V", None)])
                for q_ in range(2):
                    b.emit("s", lambda s, o=rqq[q_][:, 2:3], i=rqq[q_][:, 0:1]:
                           s.activation(out=o, in_=i, func=AF.Exp, scale=-1.0),
                           inc="Sc", waits=[("V", mxv)])
                for q_ in range(2):
                    b.emit("v", lambda v, o=rqq[q_][:, 2:3], i=rqq[q_][:, 2:3]:
                           v.tensor_scalar_add(o, i, 1.0),
                           inc="V", waits=[("Sc", None), ("V", None)])
                for q_ in range(2):
                    b.emit("v", lambda v, o=rqq[q_][:, 1:2], i=rqq[q_][:, 2:3]:
                           v.reciprocal(o, i), inc="V", waits=[("V", None)])
                for q_ in range(2):
                    b.emit("v", lambda v, o=onz[q_][:, :], i=onz[q_][:, :], m=rqq[q_][:, 1:2]:
                           v.tensor_scalar_mul(o, i, m),
                           inc="V", waits=[("V", None)])
                for q_ in range(2):
                    b.emit("t", lambda t, o=trB[q_], i=onz[q_][:, :], idd=ident[:, :]:
                           t.transpose(o, i, idd), inc="T", waits=[("V", None)])
                for q_ in range(2):
                    b.emit("v", lambda v, o=amr[q_][:, :], i=trB[q_]:
                           v.tensor_copy(o, i), inc="V", waits=[("T", None), ("DS", None)])
                for q_ in range(2):
                    b.emit("y", lambda y, o=am_d[0:E, tb * T1 + q_ * 128: tb * T1 + (q_ + 1) * 128],
                           i=amr[q_][:, :]: y.dma_start(out=o, in_=i),
                           inc="DS", k=16, waits=[("V", None)])

                def mk_proj(tb=tb, xr_t=xr_t, rope_d=rope_d):
                  def emit_proj():
                    for dq in range(CC):
                        half = dq % 2
                        osl = slice(half * 512, half * 512 + T1)
                        for c in range(CC):
                            w0 = []
                            if c == 0:
                                w0 = [("G", rope_d), ("DG", WCAST_DG), pj_prev[half]]
                            b.emit("t", lambda t, o=ps_pj[:, osl],
                                   l=wq_sb[:, c * D + dq * 128: c * D + (dq + 1) * 128],
                                   r=xr_t[:, csl[c]], st=(c == 0), sp=(c == CC - 1):
                                   t.matmul(o, l, r, start=st, stop=sp),
                                   inc=("T" if c == CC - 1 else None), waits=w0)
                        b.emit("s", lambda s, o=q_sb[:, dq * S + tb * T1: dq * S + (tb + 1) * T1],
                               i=ps_pj[:, osl]: s.copy(o, i),
                               inc="Sc", waits=[("T", None)])
                        pj_prev[half] = ("Sc", b.now("Sc"))
                    for g2 in range(NG):
                        half = g2 % 2
                        osl = slice(half * 512, half * 512 + T1)
                        for c in range(CC):
                            w0 = [pj_prev[half]] if c == 0 else []
                            b.emit("t", lambda t, o=ps_pj[0:64, osl],
                                   l=wk_sb[:, c * (NG * DK) + g2 * 64: c * (NG * DK) + (g2 + 1) * 64],
                                   r=xr_t[:, csl[c]], st=(c == 0), sp=(c == CC - 1):
                                   t.matmul(o, l, r, start=st, stop=sp),
                                   inc=("T" if c == CC - 1 else None), waits=w0)
                        b.emit("s", lambda s, o=k_sb[0:64, g2 * S + tb * T1: g2 * S + (tb + 1) * T1],
                               i=ps_pj[0:64, osl]: s.copy(o, i),
                               inc="Sc", waits=[("T", None)])
                        pj_prev[half] = ("Sc", b.now("Sc"))
                    for tt in range(T1 // 128):
                        gt = tb * (T1 // 128) + tt
                        for c in range(CC):
                            b.emit("t", lambda t, o=ps_v[:, 0:256],
                                   l=xr_t[:, c * T1 + tt * 128: c * T1 + (tt + 1) * 128],
                                   r=wv_sb[:, c * 256:(c + 1) * 256],
                                   st=(c == 0), sp=(c == CC - 1):
                                   t.matmul(o, l, r, start=st, stop=sp),
                                   inc=("T" if c == CC - 1 else None),
                                   waits=([("G", rope_d), ("V", b.now("V"))] if c == 0 else []))
                        b.emit("v", lambda v,
                               o=v_sb[:, gt * VW:(gt + 1) * VW].rearrange("p (g w) -> p g w", g=NG)[:, :, 0:64],
                               i=ps_v[:, 0:256].rearrange("p (g w) -> p g w", g=NG):
                               v.tensor_copy(o, i),
                               inc="V", waits=[("T", None), ("G", VONE_G)])
                  return emit_proj
                if pending_proj is not None:
                    pending_proj()
                pending_proj = mk_proj()
                ud["T"] = b.now("T")
                ud["Sc"] = b.now("Sc")
                prev_s2b = s2_d
                scv_prev = scv
                outb_prev = outb_dg
            pending_proj()
            b.emit("y", lambda y: y.dma_start(out=k_sb[64:128, :], in_=k_sb[0:64, :]),
                   inc="DS", k=16, waits=[("Sc", None)])
            if expose_scratch >= 2:
                b.emit("y", lambda y: y.dma_start(out=qdbg, in_=q_sb[:]),
                       inc="DS", k=16, waits=[("Sc", None), ("V", None), ("DS", None)])
                b.emit("y", lambda y: y.dma_start(out=kdbg, in_=k_sb[:]),
                       inc="DS", k=16)
                b.emit("y", lambda y: y.dma_start(out=vdbg, in_=v_sb[:]),
                       inc="DS", k=16)
        p1c = b.snap()

    if 2 in phases:
        with ExitStack() as s2:
            SB2 = lambda n, sh, d2: s2.enter_context(nc.sbuf_tensor(n, sh, d2))
            PS2 = lambda n, sh: s2.enter_context(nc.psum_tensor(n, sh, F32))
            wo_sb = SB2("wo_sb", [128, CC * D], BF16)
            p_sb = [SB2(f"p_sb{i}", [128, 1024], BF16) for i in range(3)]
            dmask = [SB2(f"dmask{i}", [128, 1024], BF16) for i in range(4)]
            rec_row = SB2("rec_row", [1, CC * 1024], F32)
            o_qb = SB2("o_qb", [128, CC * 512], BF16)
            at_sb = SB2("at_sb", [128, CC * 512], F32)
            s_ps = [PS2(f"s_ps{i}", [128, 1024]) for i in range(3)]
            o_ps = PS2("o_ps", [65, 1024])
            ps_bc = s_ps[0]

            if p1c is not None:
                pw = [("T", p1c["T"]), ("Sc", p1c["Sc"]), ("V", p1c["V"]),
                      ("G", p1c["G"])]
                b.emit("g", lambda g: g.nop(), waits=pw)
                b.emit("y", lambda y: y.nop(), waits=pw)
            b.emit("g", lambda g: g.dma_start(
                out=wo_sb[:].rearrange("p (c d) -> p c d", d=D),
                in_=woTp.rearrange("(c p) d -> p c d", p=128)), inc="DG", k=16)
            WO_DG = b.now("DG")
            for j in range(4):
                b.emit("g", lambda g, m=dmask[j][:]: g.memset(m, 1.0), inc="G")
                b.emit("g", lambda g, m=dmask[j][:], base=-128 * j:
                       g.affine_select(out=m, in_=m, pattern=[[0, 2], [1, 512]],
                                       compare_op=ALU.is_ge, fill=0.0,
                                       base=base, channel_multiplier=-1),
                       inc="G", waits=[("G", None)])
            DMASK_G = b.now("G")

            if p1c is None:
                p1c = dict(T=0, Sc=0, V=0, DS=0)
            p1T, p1Sc, p1V, p1DS = p1c["T"], p1c["Sc"], p1c["V"], p1c["DS"]
            oq_ev = b.now("V")
            wo_ev = b.now("V")
            at_dma = b.now("DG")
            exp_done = [0, 0, 0]
            av_T = [0, 0, 0]
            bcfree = [("V", 0), ("V", 0)]
            epi = {"ln": 0}
            for qb in range(8):
                q0 = qb * 512
                nkv = 4 * qb + 4
                for g in range(NG):

                    for hp in range(2):
                        h0 = 4 * g + 2 * hp
                        chunk = 2 * g + hp
                        p_ready = [("Sc", 0), ("Sc", 0), ("Sc", 0)]

                        def emit_S(kv, g=g, chunk=chunk, q0=q0, qb=qb):
                            kpar = kv % 3
                            sp = s_ps[kpar]
                            pb = p_sb[kpar]
                            diag = (kv * 128 >= q0)
                            for hh in range(2):
                                w0 = []
                                if hh == 0:
                                    w0 = [("T", p1T), ("Sc", max(p1Sc, exp_done[kpar])),
                                          ("V", wo_ev), ("DS", p1DS)]
                                b.emit("t", lambda t, o=sp[:, hh * 512:(hh + 1) * 512],
                                       l=k_sb[64 * hh:64 * hh + 64, g * S + kv * 128: g * S + (kv + 1) * 128],
                                       r=q_sb[64 * hh:64 * hh + 64, chunk * S + q0: chunk * S + q0 + 512]:
                                       t.matmul(o, l, r, start=True, stop=True),
                                       inc=("T" if hh == 1 else None), waits=w0)
                            s_T = b.now("T")
                            b.emit("s", lambda s, o=pb[:, :], i=sp[:, :]:
                                   s.activation(out=o, in_=i, func=AF.Exp, scale=0.125),
                                   inc="Sc",
                                   waits=[("T", s_T), ("T", av_T[kpar])])
                            exp_done[kpar] = b.now("Sc")
                            if diag:
                                b.emit("v", lambda v, o=pb[:, :], m=dmask[kv - 4 * qb][:]:
                                       v.tensor_mul(o, o, m),
                                       inc="V", waits=[("Sc", None), ("G", DMASK_G)])
                                p_ready[kpar] = ("V", b.now("V"))
                            else:
                                p_ready[kpar] = ("Sc", exp_done[kpar])

                        def emit_AV(kv, g=g, nkv=nkv):
                            kpar = kv % 3
                            pb = p_sb[kpar]
                            for hh in range(2):
                                b.emit("t", lambda t, o=o_ps[:, hh * 512:(hh + 1) * 512],
                                       l=v_sb[:, kv * VW + g * 65: kv * VW + g * 65 + 65],
                                       r=pb[:, hh * 512:(hh + 1) * 512],
                                       st=(kv == 0), sp2=(kv == nkv - 1):
                                       t.matmul(o, l, r, start=st, stop=sp2),
                                       inc=("T" if hh == 1 else None),
                                       waits=([p_ready[kpar], ("V", oq_ev),
                                               ("Sc", epi["ln"])]
                                              if hh == 0 else []))
                            av_T[kpar] = b.now("T")

                        for kv in range(nkv):
                            emit_S(kv)
                            if kv >= 2:
                                emit_AV(kv - 2)
                        emit_AV(nkv - 2)
                        emit_AV(nkv - 1)
                        b.emit("s", lambda s, o=rec_row[0:1, chunk * 1024:(chunk + 1) * 1024],
                               i=o_ps[64:65, 0:1024]:
                               s.activation(out=o, in_=i, func=AF.Ln),
                               inc="Sc", waits=[("T", None)])
                        epi["ln"] = b.now("Sc")
                        b.emit("s", lambda s, o=rec_row[0:1, chunk * 1024:(chunk + 1) * 1024],
                               i=rec_row[0:1, chunk * 1024:(chunk + 1) * 1024]:
                               s.activation(out=o, in_=i, func=AF.Exp, scale=-1.0),
                               inc="Sc", waits=[("Sc", None)])
                        rec_sc = b.now("Sc")
                        for hh in range(2):
                            b.emit("v", lambda v,
                                   o=o_qb[64 * hh:64 * hh + 64, chunk * 512:(chunk + 1) * 512],
                                   a=o_ps[0:64, hh * 512:(hh + 1) * 512]:
                                   v.tensor_copy(o, a), inc="V", waits=[("T", None)])
                        oq_ev = b.now("V")
                for cc in range(CC):
                    parb = cc % 2
                    for hh in range(2):
                        b.emit("t", lambda t,
                               o=ps_bc[64 * hh:64 * hh + 64, parb * 512:(parb + 1) * 512],
                               l=ones_r[0:1, 0:64],
                               r=rec_row[0:1, cc * 1024 + hh * 512: cc * 1024 + (hh + 1) * 512]:
                               t.matmul(o, l, r, start=True, stop=True,
                                        skip_group_check=True),
                               inc=("T" if hh == 1 else None),
                               waits=([("Sc", rec_sc), ("Sc", exp_done[0]),
                                       bcfree[parb]] if hh == 0 else []))
                    b.emit("v", lambda v, o=o_qb[:, cc * 512:(cc + 1) * 512],
                           i=o_qb[:, cc * 512:(cc + 1) * 512],
                           m=ps_bc[:, parb * 512:(parb + 1) * 512]:
                           v.tensor_mul(o, i, m),
                           inc="V", waits=[("T", None), ("V", None)])
                    bcfree[parb] = ("V", b.now("V"))
                wo_ev = b.now("V")
                oq_all = b.now("V")
                wo_prev = [("Sc", exp_done[0]), ("Sc", exp_done[0])]
                for c in range(CC):
                    half = c % 2
                    osl = slice(half * 512, (half + 1) * 512)
                    for cin in range(CC):
                        w0 = ([("V", oq_all), ("DG", WO_DG), wo_prev[half]]
                              if cin == 0 else [])
                        b.emit("t", lambda t, o=s_ps[0][:, osl],
                               l=wo_sb[:, cin * D + c * 128: cin * D + (c + 1) * 128],
                               r=o_qb[:, cin * 512:(cin + 1) * 512],
                               st=(cin == 0), sp=(cin == CC - 1):
                               t.matmul(o, l, r, start=st, stop=sp),
                               inc=("T" if cin == CC - 1 else None), waits=w0)
                    b.emit("v", lambda v, o=at_sb[:, c * 512:(c + 1) * 512], i=s_ps[0][:, osl]:
                           v.tensor_copy(o, i),
                           inc="V", waits=[("T", None), ("DG", at_dma)])
                    wo_prev[half] = ("V", b.now("V"))
                    wo_ev = b.now("V")
                b.emit("g", lambda g2, o=out_r[:, :, slice(q0, q0 + 512)],
                       i=at_sb[:].rearrange("p (c t) -> p c t", t=512):
                       g2.dma_start(out=o, in_=i, accum_op=ALU.add),
                       inc="DG", k=16, waits=[("V", wo_ev), ("DS", p1DS)])
                at_dma = b.now("DG")
        p2c = b.snap()
    s12.close()

    if 3 in phases:
        CAPB = E * CAP // 128
        NF = FH // 128
        BIG = 1 << 20
        with ExitStack() as s3:
            SB3 = lambda n, sh, d2: s3.enter_context(nc.sbuf_tensor(n, sh, d2))
            PS3 = lambda n, sh: s3.enter_context(nc.psum_tensor(n, sh, F32))
            prevT = p2c["T"] if p2c else (p1c["T"] if p1c else 0)
            p1DS = p1c["DS"] if p1c else 0
            pw = [("T", prevT), ("Sc", b.now("Sc")), ("V", b.now("V")),
                  ("G", b.now("G"))]
            b.emit("g", lambda g: g.nop(), waits=pw)
            b.emit("y", lambda y: y.nop(), waits=pw)

            didx = SB3("didx", [128, 32], mybir.dt.int32)
            alt = SB3("alt", [128, 32], F32)
            breg = {}

            def _mkreg(g):
                breg["r"] = g.to_reg(E * CAP - 1)
                return g.nop()
            b.emit("g", _mkreg)

            with ExitStack() as sp3:
                SBp = lambda n, sh, d2: sp3.enter_context(nc.sbuf_tensor(n, sh, d2))
                amr = SBp("amr", [E, S], F32)
                ohb = SBp("ohb", [E, S], F32)
                rnk = SBp("rnk", [E, S], F32)
                amz = SBp("amz", [E, S], F32)
                drow = SBp("drow", [1, S], F32)
                alr = SBp("alr", [1, S], F32)
                ecol = SBp("ecol", [E, 1], F32)
                xf_sb = [SBp(f"xf_sb{i}", [128, 1024], BF16) for i in range(2)]
                xt_sb = SBp("xt_sb", [128, 1024], BF16)
                ps_pro = sp3.enter_context(nc.psum_tensor("ps_pro", [128, 512], F32))
                ps_x = sp3.enter_context(nc.psum_tensor("ps_x", [128, 1024], F32))

                b.emit("y", lambda y: y.dma_start(out=amr[:], in_=am_d),
                       inc="DS", k=16, waits=[("DS", p1DS)])
                am_ld = b.now("DS")
                b.emit("g", lambda g: g.iota(ecol[:], pattern=[[0, 1]], base=0,
                                             channel_multiplier=CAP,
                                             allow_small_or_imprecise_dtypes=True),
                       inc="G")
                b.emit("g", lambda g: g.memset(amz[:], 0.0), inc="G")
                pro_g = b.now("G")
                b.emit("v", lambda v: v.tensor_scalar(
                    out=ohb[:], in0=amr[:], scalar1=0.0, scalar2=None, op0=ALU.is_gt),
                    inc="V", waits=[("DS", am_ld)])
                b.emit("v", lambda v: v.tensor_tensor_scan(
                    rnk[:], ohb[:], amz[:], 0.0, ALU.add, ALU.add),
                    inc="V", waits=[("V", None), ("G", pro_g)])
                b.emit("v", lambda v: v.tensor_sub(rnk[:], rnk[:], ohb[:]),
                       inc="V", waits=[("V", None)])
                b.emit("v", lambda v: v.tensor_scalar(
                    out=amz[:], in0=rnk[:], scalar1=float(CAP), scalar2=None,
                    op0=ALU.is_ge), inc="V", waits=[("V", None)])
                b.emit("v", lambda v: v.tensor_scalar(
                    out=rnk[:], in0=rnk[:], scalar1=ecol[:, 0:1], scalar2=None,
                    op0=ALU.add), inc="V", waits=[("V", None)])
                b.emit("v", lambda v: v.scalar_tensor_tensor(
                    out=rnk[:], in0=amz[:], scalar=float(BIG), in1=rnk[:],
                    op0=ALU.mult, op1=ALU.add), inc="V", waits=[("V", None)])
                b.emit("v", lambda v: v.tensor_mul(ohb[:], ohb[:], rnk[:]),
                       inc="V", waits=[("V", None)])
                for srcb, dst in ((ohb, drow), (amr, alr)):
                    for c in range(CC):
                        cs2 = slice(c * 512, (c + 1) * 512)
                        b.emit("t", lambda t, o=ps_pro[0:1, :], l=ones_f[0:E, 0:1],
                               r=srcb[:, cs2]:
                               t.matmul(o, l, r, start=True, stop=True),
                               inc="T", waits=[("V", None)])
                        b.emit("v", lambda v, o=dst[0:1, cs2], i=ps_pro[0:1, :]:
                               v.tensor_copy(o, i), inc="V", waits=[("T", None)])
                for tt in range(32):
                    b.emit("t", lambda t, o=ps_x[:, tt:tt + 1],
                           l=drow[0:1, tt * 128:(tt + 1) * 128], r=ones_f[0:1, 0:1],
                           st=(tt == 0), sp=(tt == 31):
                           t.matmul(o, l, r, start=st, stop=sp),
                           inc=("T" if tt == 31 else None),
                           waits=([("V", None)] if tt == 0 else []))
                for tt in range(32):
                    b.emit("t", lambda t, o=ps_x[:, 512 + tt:512 + tt + 1],
                           l=alr[0:1, tt * 128:(tt + 1) * 128], r=ones_f[0:1, 0:1],
                           st=(tt == 0), sp=(tt == 31):
                           t.matmul(o, l, r, start=st, stop=sp),
                           inc=("T" if tt == 31 else None), waits=[])
                b.emit("v", lambda v: v.tensor_copy(didx[:], ps_x[:, 0:32]),
                       inc="V", waits=[("T", None)])
                b.emit("v", lambda v: v.tensor_copy(alt[:], ps_x[:, 512:544]),
                       inc="V", waits=[("T", None)])
                idx_v = b.now("V")
                b.emit("g", lambda g: g.memset(xt_sb[:], 0.0), inc="G")
                zg = b.now("G")
                for r in range(CAPB):
                    b.emit("g", lambda g, o=xs_d[r * 128:(r + 1) * 128, :]:
                           g.dma_start(out=o, in_=xt_sb[:]),
                           inc="DG", k=16, waits=([("G", zg)] if r == 0 else []))
                zs_dg = b.now("DG")
                xf_users = [0, 0]
                for tt in range(32):
                    par = tt % 2
                    b.emit("y", lambda y, o=xf_sb[par][:].rearrange("p (c t) -> p c t", t=128),
                           i=xhat_r[:, :, slice(tt * 128, (tt + 1) * 128)]:
                           y.dma_start(out=o, in_=i),
                           inc="DS", k=16, waits=[("T", xf_users[par]), ("DS", p1DS)])
                    xf_ld = b.now("DS")
                    for cc in range(CC):
                        half = cc // 4
                        b.emit("t", lambda t,
                               o=ps_x[:, half * 512 + (cc % 4) * 128: half * 512 + (cc % 4 + 1) * 128],
                               l=xf_sb[par][:, cc * 128:(cc + 1) * 128], r=ident_bf[:],
                               st=(cc % 4 == 0), sp=(cc % 4 == 3):
                               t.matmul(o, l, r, start=st, stop=sp),
                               inc=("T" if cc % 4 == 3 else None),
                               waits=([("DS", xf_ld), ("V", None), ("V", CONST_V)]
                                      if cc == 0 else []))
                    xf_users[par] = b.now("T")
                    b.emit("v", lambda v: v.tensor_copy(xt_sb[:], ps_x[:, 0:1024]),
                           inc="V", waits=[("T", None), ("DG", None)])
                    b.emit("g", lambda g, ix=didx[:, tt:tt + 1]:
                           g.indirect_dma_start(
                               out=xs_d, out_offset=bass.IndirectOffsetOnAxis(ix, 0),
                               in_=xt_sb[:], in_offset=None,
                               bounds_check=breg["r"], oob_is_err=False),
                           inc="DG", k=16,
                           waits=[("V", None), ("DG", zs_dg), ("V", idx_v)])
                scat_dg = b.now("DG")

            sl = ExitStack()
            SB3 = lambda n, sh, d2: sl.enter_context(nc.sbuf_tensor(n, sh, d2))
            w1s = [SB3(f"w1s{i}", [128, CC * FH], BF16) for i in range(2)]
            w2s = [SB3(f"w2s{i}", [128, NF * D], BF16) for i in range(2)]
            term_acc = SB3("term_acc", [128, CC * CAP], F32)
            h_sb = SB3("h_sb", [128, CC * CAP], BF16)
            xst = SB3("xst", [128, 6 * 1024], BF16)
            a_sb = SB3("a_sb", [128, NF * 512], BF16)
            ttm = [SB3(f"ttm{i}", [128, 1024], BF16) for i in range(2)]
            g_sb = SB3("g_sb", [128, CC], F32)
            be_sb = SB3("be_sb", [128, CC], F32)
            b1e = SB3("b1e", [128, F // 128], F32)
            a_ps = PS3("a_ps", [128, 1024])
            y_ps = PS3("y_ps", [128, 2048])

            def load_unit(u, wpar, wait_T):
                e1, fh1 = u // 2, u % 2
                for c in range(CC):
                    b.emit("g", lambda g2, o=w1s[wpar][:, c * FH:(c + 1) * FH],
                           i=w1d[e1, c * 128:(c + 1) * 128, fh1 * FH:(fh1 + 1) * FH]:
                           g2.dma_start(out=o, in_=i),
                           inc="DG", k=16, waits=([("T", wait_T)] if c == 0 else []))
                for f in range(NF):
                    b.emit("g", lambda g2, o=w2s[wpar][:, f * D:(f + 1) * D],
                           i=w2d[e1, fh1 * FH + f * 128: fh1 * FH + (f + 1) * 128, 0:D]:
                           g2.dma_start(out=o, in_=i), inc="DG", k=16)
                return b.now("DG")

            wready = [0, 0]
            wready[0] = load_unit(0, 0, 0)
            unit_endT = {}
            ev_prev = {0: ("Sc", 0), 1: ("V", 0)}
            y_bank_free = [("V", 0)] * 4
            xst_state = {"users": 0}
            tacc_tr_T = 0
            term_st = b.now("DS")
            TBL = ((0, 512), (512, 256))
            hnext = {"h_v": 0, "vec_ds": 0}

            def emit_xst_h(e2, prev_hv):
                b.emit("y", lambda y, o=g_sb[:, :], i=gamT[e2]: y.dma_start(out=o, in_=i),
                       inc="DS", k=16, waits=[("G", None), ("V", prev_hv)])
                b.emit("y", lambda y, o=be_sb[:, :], i=betT[e2]: y.dma_start(out=o, in_=i),
                       inc="DS", k=16)
                b.emit("y", lambda y, o=b1e[:, :], i=b1T[e2]: y.dma_start(out=o, in_=i),
                       inc="DS", k=16, waits=[("Sc", None), ("V", None)])
                vec_ds = b.now("DS")
                b.emit("y", lambda y, o=xst[:].rearrange("p (a c) -> p a c", c=D),
                       i=xs_d[e2 * CAP:(e2 + 1) * CAP, :].rearrange("(a p) c -> p a c", p=128):
                       y.dma_start(out=o, in_=i),
                       inc="DS", k=16, waits=[("T", xst_state["users"]), ("DG", scat_dg)])
                xst_ld = b.now("DS")
                h_war_T = b.now("T")
                for cc in range(CC):
                    for grp in range(2):
                        half = (2 * cc + grp) % 2
                        for tq3 in range(3):
                            tq = grp * 3 + tq3
                            b.emit("t", lambda t,
                                   o=a_ps[:, half * 512 + tq3 * 128: half * 512 + (tq3 + 1) * 128],
                                   l=xst[:, tq * 1024 + cc * 128: tq * 1024 + (cc + 1) * 128],
                                   r=ident_bf[:],
                                   st=(tq3 == 0), sp=(tq3 == 2):
                                   t.matmul(o, l, r, start=st, stop=sp),
                                   inc=("T" if tq3 == 2 else None),
                                   waits=([("DS", xst_ld), ev_prev[half], ("V", CONST_V)]
                                          if tq3 == 0 else []))
                        b.emit("v", lambda v,
                               o=h_sb[:, cc * CAP + grp * 384: cc * CAP + (grp + 1) * 384],
                               i=a_ps[:, half * 512: half * 512 + 384],
                               sc=g_sb[:, cc:cc + 1], bi=be_sb[:, cc:cc + 1]:
                               v.tensor_scalar(out=o, in0=i, scalar1=sc, scalar2=bi,
                                               op0=ALU.mult, op1=ALU.add),
                               inc="V", waits=[("T", None), ("DS", vec_ds),
                                               ("T", h_war_T)])
                        ev_prev[half] = ("V", b.now("V"))
                xst_state["users"] = b.now("T")
                return {"h_v": b.now("V"), "vec_ds": vec_ds}

            hnext = emit_xst_h(0, 0)
            for u in range(NU):
                e, fh = u // 2, u % 2
                wpar = u % 2
                if u + 1 < NU:
                    wready[(u + 1) % 2] = load_unit(u + 1, (u + 1) % 2,
                                                    unit_endT.get(u - 1, 0))
                if fh == 0:
                    hcur = hnext
                for i_tb, (off, tbn) in enumerate(TBL):
                    for f in range(NF):
                        fp = f % 2
                        asl = slice(fp * 512, fp * 512 + tbn)
                        for c in range(CC):
                            w0 = []
                            if c == 0:
                                w0 = [("V", hcur["h_v"]), ("DG", wready[wpar]), ev_prev[fp]]
                            b.emit("t", lambda t, o=a_ps[:, asl],
                                   l=w1s[wpar][:, c * FH + f * 128: c * FH + (f + 1) * 128],
                                   r=h_sb[:, c * CAP + off: c * CAP + off + tbn],
                                   st=(c == 0), sp=(c == CC - 1):
                                   t.matmul(o, l, r, start=st, stop=sp),
                                   inc=("T" if c == CC - 1 else None), waits=w0)
                        aT = b.now("T")
                        if f % 2 == 0:
                            b.emit("s", lambda s, o=a_sb[:, f * 512: f * 512 + tbn],
                                   i=a_ps[:, asl],
                                   bi=b1e[:, fh * NF + f: fh * NF + f + 1]:
                                   s.activation(out=o, in_=i, func=AF.Relu, bias=bi),
                                   inc="Sc", waits=[("T", aT), ("DS", hcur["vec_ds"])])
                            ev_prev[fp] = ("Sc", b.now("Sc"))
                        else:
                            b.emit("v", lambda v, o=a_sb[:, f * 512: f * 512 + tbn],
                                   i=a_ps[:, asl],
                                   bi=b1e[:, fh * NF + f: fh * NF + f + 1]:
                                   v.tensor_scalar(out=o, in0=i, scalar1=bi, scalar2=0.0,
                                                   op0=ALU.add, op1=ALU.max),
                                   inc="V", waits=[("T", aT), ("DS", hcur["vec_ds"])])
                            ev_prev[fp] = ("V", b.now("V"))
                        ev_sc = b.now("Sc")
                        ev_v = b.now("V")
                    if fh == 1 and i_tb == len(TBL) - 1 and e + 1 < E:
                        hnext = emit_xst_h(e + 1, hcur["h_v"])
                    for cc2 in range(CC):
                        bank = cc2 % 4
                        for f in range(NF):
                            w0 = []
                            if f == 0:
                                w0 = [("Sc", ev_sc), ("V", ev_v), y_bank_free[bank]]
                            b.emit("t", lambda t, o=y_ps[:, bank * 512: bank * 512 + tbn],
                                   l=w2s[wpar][:, f * D + cc2 * 128: f * D + (cc2 + 1) * 128],
                                   r=a_sb[:, f * 512: f * 512 + tbn],
                                   st=(f == 0), sp=(f == NF - 1):
                                   t.matmul(o, l, r, start=st, stop=sp),
                                   inc=("T" if f == NF - 1 else None), waits=w0)
                        yT = b.now("T")
                        if fh == 0:
                            b.emit("v", lambda v,
                                   o=term_acc[:, cc2 * CAP + off: cc2 * CAP + off + tbn],
                                   i=y_ps[:, bank * 512: bank * 512 + tbn]:
                                   v.tensor_copy(o, i),
                                   inc="V", waits=[("T", yT), ("T", tacc_tr_T)])
                        else:
                            b.emit("v", lambda v,
                                   o=term_acc[:, cc2 * CAP + off: cc2 * CAP + off + tbn],
                                   a=term_acc[:, cc2 * CAP + off: cc2 * CAP + off + tbn],
                                   i=y_ps[:, bank * 512: bank * 512 + tbn]:
                                   v.tensor_add(o, a, i),
                                   inc="V", waits=[("T", yT), ("V", None)])
                        y_bank_free[bank] = ("V", b.now("V"))
                if fh == 1:
                    for tq in range(6):
                        tpar = tq % 2
                        for grp in range(2):
                            half = grp
                            for c4 in range(4):
                                cc = grp * 4 + c4
                                b.emit("t", lambda t,
                                       o=a_ps[:, half * 512 + c4 * 128: half * 512 + (c4 + 1) * 128],
                                       l=term_acc[:, cc * CAP + tq * 128: cc * CAP + tq * 128 + 128],
                                       r=ident[:],
                                       st=(c4 == 0), sp=(c4 == 3):
                                       t.matmul(o, l, r, start=st, stop=sp),
                                       inc=("T" if c4 == 3 else None),
                                       waits=([("V", None), ev_prev[half]]
                                              if c4 == 0 else []))
                            b.emit("v", lambda v, o=ttm[tpar][:, half * 512:(half + 1) * 512],
                                   i=a_ps[:, half * 512:(half + 1) * 512]:
                                   v.tensor_copy(o, i),
                                   inc="V", waits=[("T", None), ("DS", term_st)])
                            ev_prev[half] = ("V", b.now("V"))
                        b.emit("y", lambda y, o=term_d[e * CAP + tq * 128: e * CAP + (tq + 1) * 128, :],
                               i=ttm[tpar][:]: y.dma_start(out=o, in_=i),
                               inc="DS", k=16, waits=[("V", None)])
                        term_st = b.now("DS")
                    tacc_tr_T = b.now("T")
                    h_users_T = b.now("T")
                unit_endT[u] = b.now("T")

            sl.close()
            uw = [("T", None), ("Sc", None), ("V", None), ("DS", None),
                  ("DG", None)]
            b.emit("g", lambda g: g.nop(), waits=uw)
            b.emit("v", lambda v: v.nop(), waits=uw)
            b.emit("y", lambda y: y.nop(), waits=uw)
            with ExitStack() as su:
                SBu = lambda n, sh, d2: su.enter_context(nc.sbuf_tensor(n, sh, d2))
                gt_sb = [SBu(f"gt_sb{i}", [128, 1024], BF16) for i in range(2)]
                ff_sb = [SBu(f"ff_sb{i}", [128, 1024], F32) for i in range(2)]
                am2 = SBu("am2", [E, T3], F32)
                y_read = [0, 0]
                ff_dma = [b.now("DG")] * 2
                gt_mul = [0, 0]
                for tt in range(32):
                    up = tt % 2
                    b.emit("g", lambda g, o=gt_sb[up][:]: g.memset(o, 0.0),
                           inc="G", waits=[("V", gt_mul[up]), ("T", y_read[up])])
                    gt_g = b.now("G")
                    b.emit("g", lambda g, ix=didx[:, tt:tt + 1], o=gt_sb[up][:]:
                           g.indirect_dma_start(
                               out=o, out_offset=None,
                               in_=term_d, in_offset=bass.IndirectOffsetOnAxis(ix, 0),
                               bounds_check=breg["r"], oob_is_err=False),
                           inc="DG", k=16,
                           waits=[("DS", term_st), ("G", gt_g)])
                    gt_dg = b.now("DG")
                    b.emit("v", lambda v, m=alt[:, tt:tt + 1], o=gt_sb[up][:]:
                           v.tensor_scalar_mul(o, o, m),
                           inc="V", waits=[("DG", gt_dg), ("T", y_read[up])])
                    gf_v = b.now("V")
                    gt_mul[up] = gf_v
                    for cc in range(CC):
                        half = cc // 4
                        bank2 = up * 2 + half
                        b.emit("t", lambda t,
                               o=y_ps[:, bank2 * 512 + (cc % 4) * 128: bank2 * 512 + (cc % 4 + 1) * 128],
                               l=gt_sb[up][:, cc * 128:(cc + 1) * 128], r=ident_bf[:],
                               st=(cc % 4 == 0), sp=(cc % 4 == 3):
                               t.matmul(o, l, r, start=st, stop=sp),
                               inc=("T" if cc % 4 == 3 else None),
                               waits=([("V", gf_v), y_bank_free[up * 2],
                                       y_bank_free[up * 2 + 1]]
                                      if cc == 0 else []))
                    y_read[up] = b.now("T")
                    b.emit("v", lambda v, o=ff_sb[up][:],
                           i=y_ps[:, up * 1024:(up + 1) * 1024]:
                           v.tensor_copy(o, i),
                           inc="V", waits=[("T", None), ("DG", ff_dma[up])])
                    y_bank_free[up * 2] = ("V", b.now("V"))
                    y_bank_free[up * 2 + 1] = ("V", b.now("V"))
                    b.emit("g", lambda g2, o=out_r[:, :, slice(tt * 128, (tt + 1) * 128)],
                           i=ff_sb[up][:].rearrange("p (c t) -> p c t", t=128):
                           g2.dma_start(out=o, in_=i, accum_op=ALU.add),
                           inc="DG", k=16, waits=[("V", None)])
                    ff_dma[up] = b.now("DG")
                bt_dma = [b.now("DG")] * 2
                bt_ev = [("V", b.now("V")), ("V", b.now("V"))]
                for tb in range(NT3):
                    tsl = slice(tb * T3, (tb + 1) * T3)
                    b.emit("y", lambda y, o=am2[:, :], i=am_d[0:E, tsl]:
                           y.dma_start(out=o, in_=i),
                           inc="DS", k=16, waits=[("T", None), ("V", None)])
                    am2_ld = b.now("DS")
                    for c in range(CC):
                        bb2 = c % 2
                        b.emit("t", lambda t, o=a_ps[:, bb2 * 512:bb2 * 512 + T3],
                               l=b2_sb[0:E, c * 128:(c + 1) * 128],
                               r=am2[:, :]: t.matmul(o, l, r, start=True, stop=True),
                               inc="T",
                               waits=[("DS", max(am2_ld, CONST_DS)), bt_ev[bb2]])
                        b.emit("v", lambda v, o=ff_sb[c % 2][:, 0:T3],
                               i=a_ps[:, bb2 * 512:bb2 * 512 + T3]:
                               v.tensor_copy(o, i), inc="V",
                               waits=[("T", None), ("DG", bt_dma[c % 2])])
                        bt_ev[bb2] = ("V", b.now("V"))
                        b.emit("g", lambda g2,
                               o=out_r[:, c:c + 1, tsl],
                               i=ff_sb[c % 2][:, 0:T3].rearrange("p (o t) -> p o t", o=1):
                               g2.dma_start(out=o, in_=i, accum_op=ALU.add),
                               inc="DG", k=16, waits=[("V", None)])
                        bt_dma[c % 2] = b.now("DG")

    b.emit("g", lambda g2: g2.nop(), waits=[("DG", None), ("DS", None)])
    b.emit("y", lambda y: y.nop(), waits=[("DG", None), ("DS", None)])

    b.flush(blk)
    es.close()
    return nc


_PERM = np.concatenate([np.arange(0, D, 2), np.arange(1, D, 2)])


def host_inputs(L0_attn_X, L0_attn_W_q, L0_attn_W_k, L0_attn_W_v, L0_attn_W_o,
                L0_ffn_centroids, L0_ffn_gamma, L0_ffn_beta, L0_ffn_W1,
                L0_ffn_b1, L0_ffn_W2, L0_ffn_b2):
    f32 = np.float32
    perm = _PERM
    X = np.asarray(L0_attn_X, f32)
    theta = (1.0 / 10000.0 ** (np.arange(D // 2, dtype=np.float64) / (D // 2)))
    ang = np.arange(S, dtype=np.float64)[:, None] * theta[None, :]
    cosT = np.ascontiguousarray(np.cos(ang).T.astype(f32))
    sinT = np.ascontiguousarray(np.sin(ang).T.astype(f32))

    Wq = np.asarray(L0_attn_W_q, f32)
    Wk = np.asarray(L0_attn_W_k, f32)
    Wv = np.asarray(L0_attn_W_v, f32)
    Wo = np.asarray(L0_attn_W_o, f32)
    cen = np.asarray(L0_ffn_centroids, f32)
    gam = np.asarray(L0_ffn_gamma, f32)
    bet = np.asarray(L0_ffn_beta, f32)
    W1 = np.asarray(L0_ffn_W1, f32)
    b1 = np.asarray(L0_ffn_b1, f32)
    W2 = np.asarray(L0_ffn_W2, f32)
    b2 = np.asarray(L0_ffn_b2, f32)

    shared = {
        "wqT": np.ascontiguousarray(Wq.T),
        "wkT": np.ascontiguousarray(Wk.T),
        "wvT": np.ascontiguousarray(Wv.T),
        "woTp": np.ascontiguousarray(Wo.T[:, perm]),
        "cenTp": np.ascontiguousarray(cen.T[perm]),
        "gamT": np.ascontiguousarray(gam[:, perm].reshape(E, CC, 128).transpose(0, 2, 1)),
        "betT": np.ascontiguousarray(bet[:, perm].reshape(E, CC, 128).transpose(0, 2, 1)),
        "b1T": np.ascontiguousarray(b1.reshape(E, F // 128, 128).transpose(0, 2, 1)),
        "b2fm": np.ascontiguousarray(b2[:, perm]),
        "w1d": np.ascontiguousarray(W1.transpose(0, 2, 1)[:, perm, :]),
        "w2d": np.ascontiguousarray(W2.transpose(0, 2, 1)[:, :, perm]),
        "cosT": cosT,
        "sinT": sinT,
    }
    in_maps = []
    for bb in range(NC):
        m = dict(shared)
        m["xTp"] = np.ascontiguousarray(X[bb].T[perm])
        in_maps.append(m)
    return in_maps


def kernel(**inputs):
    in_maps = host_inputs(**inputs)
    nc = build_program()
    res = run_bass_kernel_spmd(nc, in_maps, core_ids=list(range(NC)))
    iperm = np.argsort(_PERM)
    outs = [np.asarray(r["out"])[iperm].T for r in res.results]
    return np.stack(outs).astype(np.float32)
